# version 04
# speedup vs baseline: 1.0828x; 1.0290x over previous
import os
import contextlib
import numpy as np
import ml_dtypes
import concourse.bass as bass
import concourse.mybir as mybir
from concourse.bass_utils import run_bass_kernel_spmd

F32 = mybir.dt.float32
BF16 = mybir.dt.bfloat16
U8 = mybir.dt.uint8
I32 = mybir.dt.int32
FP8 = mybir.dt.float8e4
AF = mybir.ActivationFunctionType
ALU = mybir.AluOpType
AX = mybir.AxisListType

LN2 = float(np.log(2.0))
EPS = 1e-5
S = 8192
D = 1024
T = 256
W = 128
NT = S // T
NT_DEV = int(os.environ.get('K_NT', NT))
K_CUT = int(os.environ.get('K_CUT', 99))
K_GCUT = int(os.environ.get('K_GCUT', 99))
NCH = D // 128
NH0, HD = 16, 64
NH1 = 8
ETA, DTC = 0.1, 0.01
INV_D = 1.0 / D
NCHUNK = S // 128


class P:

    def __init__(self):
        self.nc = bass.Bass("TRN2", target_bir_lowering=False,
                    detect_race_conditions=False)
        self.ctx = contextlib.ExitStack()
        self.scopes = [self.ctx]
        self.sems = {}
        self.cnt = {}
        self.buf = []
        self.curr = None
        self.blk = None
        self.need = {'A': {}, 'B': {}, 'C': {}, 'D': {}}

    def open_block(self):
        self._blk_cm = self.nc.Block()
        self.blk = self._blk_cm.__enter__()

    def close_block(self):
        self._flush_group()
        self._blk_cm.__exit__(None, None, None)
        self.blk = None
        self.curr = None

    def push_scope(self):
        st = contextlib.ExitStack()
        st.__enter__()
        self.scopes.append(st)

    def pop_scope(self):
        self.scopes.pop().close()

    def sb(self, name, shape, dt=F32):
        return self.scopes[-1].enter_context(self.nc.sbuf_tensor(name, shape, dt))[:]

    def ps(self, name, shape, dt=F32):
        return self.ctx.enter_context(self.nc.psum_tensor(name, shape, dt))[:]

    def sem(self, name):
        if name not in self.sems:
            self.sems[name] = self.ctx.enter_context(self.nc.semaphore(name))
            self.cnt.setdefault(name, 0)
        return self.sems[name]

    def inc(self, name, k=1):
        self.sem(name)
        self.cnt[name] += k
        return self.sems[name], self.cnt[name]

    def val(self, name):
        self.sem(name)
        self._flush_group()
        return self.cnt[name]

    def on(self, eng, fn):
        if self.curr != eng:
            self._flush_group()
            self.curr = eng
        self.buf.append(fn)

    def _flush_group(self):
        if self.buf:
            fns = tuple(self.buf)
            self.buf = []
            getattr(self.blk, self.curr)(lambda e, fns=fns: [f(e) for f in fns])

    def Te(self, fn): self.on('tensor', fn)
    def Ve(self, fn): self.on('vector', fn)
    def Ae(self, fn): self.on('scalar', fn)
    def Ge(self, fn): self.on('gpsimd', fn)
    def Sy(self, fn): self.on('sync', fn)

    def wait(self, eng_name, sem_name, v=None):
        h = self.sem(sem_name)
        if v is None:
            def f(e, h=h, n=sem_name):
                if self.cnt[n] > 0:
                    e.wait_ge(h, self.cnt[n])
            self.on(eng_name, f)
        else:
            if v > 0:
                self.on(eng_name, lambda e, h=h, v=v: e.wait_ge(h, v))

    def mark(self, pool, sem_name):
        self.sem(sem_name)
        self._flush_group()
        self.need[pool][sem_name] = self.cnt[sem_name]

    def guard(self, pool, eng='tensor'):
        self._flush_group()
        for s, v in self.need[pool].items():
            self.wait(eng, s, v)
        self.need[pool] = {}

    def join(self, engines=('tensor', 'vector', 'scalar')):
        sems = ('sP', 'sV', 'sA')
        for e in engines:
            def f(eng):
                for s in sems:
                    if self.cnt.get(s, 0) > 0:
                        eng.wait_ge(self.sems[s], self.cnt[s])
            self.on(e, f)



def alloc_consts(p):
    c = {}
    c['IDBF'] = p.sb('IDBF', [128, 128], BF16)
    c['IDF32'] = p.sb('IDF32', [128, 128], F32)
    c['LSN'] = p.sb('LSN', [128, 128], F32)
    c['CAUS'] = p.sb('CAUS', [128, 128], F32)
    c['ONESROW'] = p.sb('ONESROW', [1, 128], F32)
    c['ONESCOLF'] = p.sb('ONESCOLF', [128, 1], F32)
    c['ONESROWB'] = p.sb('ONESROWB', [1, 128], BF16)
    c['FOLD'] = p.sb('FOLD', [128, 64], BF16)
    c['EPSC'] = p.sb('EPSC', [128, 1], F32)
    c['ONEC'] = p.sb('ONEC', [128, 1], F32)
    return c


def emit_consts(p, c):
    def f(g):
        g.memset(c['IDBF'], 0.0)
        g.affine_select(out=c['IDBF'], in_=c['IDBF'], compare_op=ALU.not_equal,
                        fill=1.0, base=0, pattern=[[-1, 128]], channel_multiplier=1)
        g.memset(c['IDF32'], 0.0)
        g.affine_select(out=c['IDF32'], in_=c['IDF32'], compare_op=ALU.not_equal,
                        fill=1.0, base=0, pattern=[[-1, 128]], channel_multiplier=1)
        g.memset(c['LSN'], -1.0)
        g.affine_select(out=c['LSN'], in_=c['LSN'], compare_op=ALU.is_gt,
                        fill=0.0, base=0, pattern=[[-1, 128]], channel_multiplier=1)
        g.memset(c['CAUS'], 1.0)
        g.affine_select(out=c['CAUS'], in_=c['CAUS'], compare_op=ALU.is_ge,
                        fill=0.0, base=0, pattern=[[1, 128]], channel_multiplier=-1)
        g.memset(c['ONESROW'], 1.0)
        g.memset(c['ONESCOLF'], 1.0)
        g.memset(c['FOLD'], 0.0)
        g.affine_select(out=c['FOLD'], in_=c['FOLD'], compare_op=ALU.not_equal,
                        fill=1.0, base=0, pattern=[[-1, 64]], channel_multiplier=1)
        g.affine_select(out=c['FOLD'], in_=c['FOLD'], compare_op=ALU.not_equal,
                        fill=1.0, base=-64, pattern=[[-1, 64]], channel_multiplier=1)
        g.memset(c['ONESROWB'], 1.0)
        g.memset(c['EPSC'], EPS)
        h, _ = p.inc('sG')
        g.memset(c['ONEC'], 1.0).then_inc(h, 1)
    p.Ge(f)


K_LN = float(os.environ.get('K_LN', 99))


def ln_stats_cols(p, x_w, cols, slot, arena0, epsc):
    mcol = cols[:, slot:slot + 1]
    rcol = cols[:, slot + 1:slot + 2]
    nmr = cols[:, slot + 2:slot + 3]
    vcol = cols[:, slot + 3:slot + 4]

    p.wait('scalar', 'sV')
    p.wait('vector', 'sA')

    def fv(v):
        v.tensor_tensor(arena0, x_w, x_w, ALU.mult)
        h, _ = p.inc('sV')
        v.reduce_sum(mcol, x_w, AX.X).then_inc(h, 1)
    p.Ve(fv)
    p.wait('vector', 'sV')

    def fv1b(v):
        h, _ = p.inc('sV')
        v.reduce_sum(vcol, arena0, AX.X).then_inc(h, 1)
    p.Ve(fv1b)
    p.wait('vector', 'sV')

    def fv2(v):
        v.tensor_scalar(mcol, mcol, INV_D, None, ALU.mult)
        v.tensor_scalar(vcol, vcol, INV_D, None, ALU.mult)
        h, _ = p.inc('sV')
        v.tensor_scalar(nmr, mcol, mcol, None, ALU.mult).then_inc(h, 1)
    p.Ve(fv2)
    p.wait('vector', 'sV')

    def fv2b(v):
        h, _ = p.inc('sV')
        v.tensor_scalar(vcol, vcol, nmr, None, ALU.subtract).then_inc(h, 1)
    p.Ve(fv2b)
    p.wait('scalar', 'sV')

    def fa(a):
        h, _ = p.inc('sA')
        a.activation(rcol, vcol, AF.Ln, bias=epsc).then_inc(h, 1)
    p.Ae(fa)
    p.wait('scalar', 'sA')

    def fa2(a):
        h, _ = p.inc('sA')
        a.activation(rcol, rcol, AF.Exp, scale=-0.5).then_inc(h, 1)
    p.Ae(fa2)
    p.wait('vector', 'sA')

    def fv3(v):
        h, _ = p.inc('sV')
        v.tensor_scalar(nmr, mcol, rcol, -1.0, ALU.mult,
                        ALU.mult).then_inc(h, 1)
    p.Ve(fv3)
    return dict(m=mcol, r=rcol, negmr=nmr, v=vcol)


def tm_to_fm(p, c, src_tm, dst_fm, w, pool, poolb, tile_cols=T, nch=NCH):
    p.guard('A' if pool is p.psA else 'B')

    def ft(t):
        ins = None
        for f in range(nch):
            ins = t.transpose(poolb[:, f * 128:(f + 1) * 128],
                              src_tm[:, f * 128:(f + 1) * 128], c['IDBF'])
        hh, _ = p.inc('sP')
        ins.then_inc(hh, 1)
    p.Te(ft)
    p.wait('scalar', 'sP')

    def fa(a):
        d3 = dst_fm.rearrange("q (f x) -> q f x", x=tile_cols)[
            :, 0:nch, w * 128:w * 128 + 128]
        s3 = poolb[:, 0:nch * 128].rearrange("q (f x) -> q f x", x=128)
        hh, _ = p.inc('sA')
        a.activation(d3, s3, AF.Identity).then_inc(hh, 1)
    p.Ae(fa)
    p.mark('A' if pool is p.psA else 'B', 'sA')



def emit_phase0(p, c, io, VT, GT):
    WQ = p.sb('WQ', [128, NCH * D], BF16)
    WK = p.sb('WK', [128, NCH * D], BF16)
    WV = p.sb('WV', [128, NCH * D], BF16)
    WG = p.sb('WG', [128, NCH * D], BF16)
    WO = p.sb('WO', [128, NCH * D], BF16)
    WF = p.sb('WF', [128, NCH * D], BF16)
    WGK1 = p.sb('WGK1', [128, NCH * 16], BF16)
    WGK2 = p.sb('WGK2', [16, D], BF16)
    DISHA = p.sb('DISHA', [64, D], BF16)
    BGK2R = p.sb('BGK2R', [1, D], BF16)
    GBC = p.sb('GBC', [128, D], F32)
    BBC = p.sb('BBC', [128, D], F32)
    GWBC = p.sb('GWBC', [128, 2 * D], BF16)
    BWC = p.sb('BWC', [128, 2], F32)

    X0 = p.sb('X0', [128, 2 * 2048], F32)

    XNF = p.sb('XNF', [128, NCH * T], BF16)
    QFM = p.sb('QFM', [128, NCH * T], BF16)
    KFM = p.sb('KFM', [128, NCH * T], BF16)
    KTM = p.sb('KTM', [128, 2 * D], BF16)
    VTM = p.sb('VTM', [128, 2 * D], BF16)
    GATE = p.sb('GATE', [128, 2 * D], BF16)
    Z1 = p.sb('Z1', [16, T], BF16)
    GKT = p.sb('GKT', [128, D], F32)
    VBAR = p.sb('VBAR', [128, D], BF16)
    ATM = p.sb('ATM', [128, NH0 * 128], BF16)
    SPRE = p.sb('SPRE', [64, NH0 * 64], F32)
    SST = p.sb('SST', [64, NH0 * 64], F32)
    RCB = p.sb('RCB', [128, D], BF16)
    RFM = p.sb('RFM', [128, NCH * T], BF16)
    X1 = p.sb('X1', [128, 2048], F32)

    PL1 = p.sb('PL1', [128, T], F32)
    AR = p.sb('AR', [128, 3 * 1024], F32)
    COLS = p.sb('COLS', [128, 32], F32)
    XNT = X1.bitcast(BF16)[:, 0:1024]
    SPREB = X1.bitcast(BF16)[:, 2048:2560]
    XN1 = AR.bitcast(BF16)[:, 0:1024]

    arena = [AR[:, i * 1024:(i + 1) * 1024] for i in range(3)]
    p.taps = dict(XNF=XNF, QFM=QFM, KFM=KFM, KTM=KTM, VTM=VTM, GATE=GATE,
                  GKT=GKT, ATM=ATM, RFM=RFM, X1=X1, AR=AR, COLS=COLS, PL1=PL1,
                  SPRE=SPRE, SST=SST, GWBC2=GWBC, GBC=GBC, BBC=BBC, BWC=BWC)
    A0, B0 = p.psA, p.psB
    A0b, B0b = A0.bitcast(BF16), B0.bitcast(BF16)
    x_d, x1_d = io['x'], io['x1']

    if os.environ.get('K_TAP'):
        def taps0(g):
            h, _ = p.inc('sG')
            g.memset(COLS, 0.0)
            g.memset(PL1, 0.0)
            g.memset(X1, 0.0)
            for tsr in (QFM, KFM, KTM, VTM, GATE, GKT, ATM, RFM, SPRE, XNF):
                g.memset(tsr, 0.0)
            g.memset(AR, 0.0).then_inc(h, 1)
        p.Ge(taps0)

    def wload(g):
        h = p.sem('sWD')
        for name, dst in [('wq', WQ), ('wk', WK), ('wv', WV), ('wg', WG),
                          ('wout', WO), ('wffT', WF)]:
            g.dma_start(dst.rearrange("q (c n) -> q c n", c=NCH),
                        io[name].rearrange("(c q) n -> q c n", q=128)).then_inc(h, 16)
        g.dma_start(WGK1.rearrange("q (c n) -> q c n", c=NCH),
                    io['wgk1'].rearrange("(c q) n -> q c n", q=128)).then_inc(h, 16)
        g.dma_start(WGK2, io['wgk2'][:]).then_inc(h, 16)
        g.dma_start(DISHA, io['disha'][:]).then_inc(h, 16)
        g.dma_start(BGK2R, io['bgk2r'][:]).then_inc(h, 16)
    p.sem('sWD'); p.sem('sX'); p.sem('sS')
    p.Ge(wload)
    p.cnt['sWD'] += 160

    XR0 = X0[0:1, 0:1024]
    XR1 = X0[0:1, 1024:2048]
    XR2 = X0[0:1, 2048:3072]
    XR3 = X0[0:1, 3072:4096]

    def wload2(y):
        h = p.sem('sX')
        y.dma_start(XR0, io['g0r'][:]).then_inc(h, 16)
        y.dma_start(XR1, io['b0r'][:]).then_inc(h, 16)
    p.Sy(wload2)
    p.cnt['sX'] += 32

    for e in ('tensor', 'vector', 'scalar'):
        p.wait(e, 'sG')
        p.wait(e, 'sWD')
        p.wait(e, 'sX')

    def su_pe(t):
        ins = None
        for g in range(2):
            t.matmul(A0[:, g * 512:(g + 1) * 512], c['ONESROW'],
                     XR0[0:1, g * 512:(g + 1) * 512], start=True, stop=True)
            ins = t.matmul(A0[:, 1024 + g * 512:1024 + (g + 1) * 512], c['ONESROW'],
                           XR1[0:1, g * 512:(g + 1) * 512], start=True, stop=True)
        h, _ = p.inc('sP')
        ins.then_inc(h, 1)
    p.Te(su_pe)
    p.wait('scalar', 'sP')

    def su_a(a):
        a.activation(GBC, A0[:, 0:1024], AF.Identity)
        h, _ = p.inc('sA')
        a.activation(BBC, A0[:, 1024:2048], AF.Identity).then_inc(h, 1)
    p.Ae(su_a)
    p.mark('A', 'sA')

    p.wait('sync', 'sP')
    def wload3(y):
        h = p.sem('sX')
        y.dma_start(XR0, io['wp0r'][:]).then_inc(h, 16)
        y.dma_start(XR3, io['wp1r'][:]).then_inc(h, 16)
        y.dma_start(XR1, io['b1r'][:]).then_inc(h, 16)
        y.dma_start(COLS[0:1, 0:2], io['bpotr'][:]).then_inc(h, 16)
    p.Sy(wload3)
    p.cnt['sX'] += 64
    p.wait('vector', 'sX')

    def su_v(v):
        v.tensor_tensor(XR2, XR1, XR0, ALU.mult)
        v.reduce_sum(COLS[0:1, 2:3], XR2, AX.X)
        v.tensor_tensor(XR2, XR1, XR3, ALU.mult)
        v.reduce_sum(COLS[0:1, 3:4], XR2, AX.X)
        h, _ = p.inc('sV')
        v.memset(SST, 0.0).then_inc(h, 1)
    p.Ve(su_v)
    p.wait('vector', 'sV')
    p.wait('sync', 'sV')
    def wload4(y):
        h, _ = p.inc('sX', 16)
        y.dma_start(XR1, io['g1r'][:]).then_inc(h, 16)
    p.Sy(wload4)
    p.wait('vector', 'sX')

    def su_v2(v):
        v.tensor_tensor(COLS[0:1, 4:6], COLS[0:1, 2:4],
                        COLS[0:1, 0:2], ALU.add)
        v.tensor_tensor(XR2, XR0, XR1, ALU.mult)
        h, _ = p.inc('sV')
        v.tensor_tensor(XR0, XR3, XR1, ALU.mult).then_inc(h, 1)
    p.Ve(su_v2)
    p.wait('tensor', 'sV')
    p.guard('A')

    def su_pe2(t):
        for g in range(2):
            t.matmul(A0[:, g * 512:(g + 1) * 512], c['ONESROW'],
                     XR2[0:1, g * 512:(g + 1) * 512], start=True, stop=True)
            t.matmul(A0[:, 1024 + g * 512:1024 + (g + 1) * 512], c['ONESROW'],
                     XR0[0:1, g * 512:(g + 1) * 512], start=True, stop=True)
        h, _ = p.inc('sP')
        t.matmul(B0[0:128, 0:2], c['ONESROW'],
                 COLS[0:1, 4:6], start=True, stop=True).then_inc(h, 1)
    p.Te(su_pe2)
    p.wait('scalar', 'sP')

    def su_a2(a):
        a.activation(GWBC, A0[:, 0:2048], AF.Identity)
        h, _ = p.inc('sA')
        a.activation(BWC, B0[0:128, 0:2], AF.Identity).then_inc(h, 1)
    p.Ae(su_a2)
    p.mark('A', 'sA')
    p.mark('B', 'sA')
    p.wait('sync', 'sA')

    def ld0(y):
        src = x_d[0:T, :].rearrange("(w q) f -> q w f", q=128)
        h, _ = p.inc('sX', 16)
        y.dma_start(X0[:, 0:2048].rearrange("q (w f) -> q w f", w=2),
                    src).then_inc(h, 16)
    p.Sy(ld0)

    x1add_cnt = {}
    xload_base = p.val('sX')

    for t in range(NT_DEV):
        buf = t % 2
        X0t = X0[:, buf * 2048:(buf + 1) * 2048]
        if t + 1 < NT_DEV:
            if t - 1 >= 0:
                p.wait('sync', 'sV', x1add_cnt[t - 1])
            def ldn(y, t=t):
                src = x_d[(t + 1) * T:(t + 2) * T, :].rearrange(
                    "(w q) f -> q w f", q=128)
                h, _ = p.inc('sX', 16)
                y.dma_start(X0[:, ((t + 1) % 2) * 2048:((t + 1) % 2 + 1) * 2048]
                            .rearrange("q (w f) -> q w f", w=2),
                            src).then_inc(h, 16)
            p.Sy(ldn)

        p.wait('vector', 'sX', xload_base + 16 * t)
        p.wait('scalar', 'sX', xload_base + 16 * t)
        p.wait('tensor', 'sX', xload_base + 16 * t)
        if t > 0:
            p.wait('scalar', 'sS', 16 * t)
            p.wait('vector', 'sS', 16 * t)
            p.wait('tensor', 'sS', 16 * t)

        if K_CUT >= 1:
            for w in range(2):
                xw = X0t[:, w * 1024:(w + 1) * 1024]
                st = ln_stats_cols(p, xw, COLS, 8 * w, arena[0], c['EPSC'])
                p.wait('scalar', 'sV')

                p.wait('scalar', 'sP')

                def mkxn(a, xw=xw, st=st):
                    h, _ = p.inc('sA')
                    a.activation(XNT, xw, AF.Identity, scale=st['r'],
                                 bias=st['negmr']).then_inc(h, 1)
                p.Ae(mkxn)
                p.wait('tensor', 'sA')
                tm_to_fm(p, c, XNT, XNF, w, p.psA, A0b[:, 0:1024])

        def do_proj(wsb, dst, pool, func=AF.Identity, kind='fm', dout=D):
            pn = 'A' if pool is A0 else 'B'
            p.guard(pn)
            if kind == 'fm':
                def ft(t_):
                    ins = None
                    for m in range(NCH):
                        for k in range(NCH):
                            ins = t_.matmul(pool[:, m * T:(m + 1) * T],
                                            wsb[:, k * D + m * 128:k * D + m * 128 + 128],
                                            XNF[:, k * T:(k + 1) * T],
                                            start=(k == 0), stop=(k == NCH - 1))
                    h, _ = p.inc('sP')
                    ins.then_inc(h, 1)
                p.Te(ft)
            else:
                def ft(t_):
                    ins = None
                    for w in range(2):
                        for g in range(2):
                            for k in range(NCH):
                                ins = t_.matmul(
                                    pool[:, w * D + g * 512:w * D + (g + 1) * 512],
                                    XNF[:, k * T + w * 128:k * T + w * 128 + 128],
                                    wsb[:, k * D + g * 512:k * D + (g + 1) * 512],
                                    start=(k == 0), stop=(k == NCH - 1))
                    h, _ = p.inc('sP')
                    ins.then_inc(h, 1)
                p.Te(ft)
            p.wait('scalar', 'sP')

            def ev(a, dst=dst, pool=pool, func=func, kind=kind, dout=dout):
                h, _ = p.inc('sA')
                a.activation(dst[:, 0:NCH * T if kind == 'fm' else 2 * dout],
                             pool[:, 0:NCH * T if kind == 'fm' else 2 * dout],
                             func).then_inc(h, 1)
            p.Ae(ev)
            p.mark(pn, 'sA')

        if K_CUT >= 2:
            do_proj(WQ, QFM, A0, kind='fm')
            do_proj(WK, KFM, B0, kind='fm')
            do_proj(WK, KTM, A0, kind='tm')
            do_proj(WV, VTM, B0, kind='tm')
            do_proj(WG, GATE, A0, func=AF.Silu, kind='tm')

        if K_CUT >= 3:
            p.guard('B')

            def zmm1(t_):
                ins = None
                for k in range(NCH):
                    ins = t_.matmul(B0[0:16, 0:T], WGK1[:, k * 16:(k + 1) * 16],
                                    XNF[:, k * T:(k + 1) * T],
                                    start=(k == 0), stop=(k == NCH - 1))
                h, _ = p.inc('sP')
                ins.then_inc(h, 1)
            p.Te(zmm1)
            p.wait('scalar', 'sP')

            def zev(a):
                h, _ = p.inc('sA')
                a.activation(Z1, B0[0:16, 0:T], AF.Identity).then_inc(h, 1)
            p.Ae(zev)
            p.mark('B', 'sA')
        if K_CUT >= 4:
            for w in range(2):
                pXn, pYn = ('A', 'B') if w == 0 else ('B', 'A')
                pX = A0 if w == 0 else B0
                pY = B0 if w == 0 else A0
                pYb = B0b if w == 0 else A0b
                gkc = GKT

                p.guard(pXn)
                p.wait('tensor', 'sA')

                def zmm2(t_, pX=pX, w=w):
                    ins = None
                    for g in range(2):
                        t_.matmul(pX[:, g * 512:(g + 1) * 512],
                                  Z1[:, w * 128:(w + 1) * 128],
                                  WGK2[:, g * 512:(g + 1) * 512], start=True, stop=False)
                        ins = t_.matmul(pX[:, g * 512:(g + 1) * 512],
                                        c['ONESROWB'], BGK2R[0:1, g * 512:(g + 1) * 512],
                                        start=False, stop=True)
                    h, _ = p.inc('sP')
                    ins.then_inc(h, 1)
                p.Te(zmm2)
                p.wait('scalar', 'sP')
                p.wait('vector', 'sP')

                if K_GCUT < 2:
                    continue
                def lsg_a(a, pX=pX):
                    a.activation(arena[1], pX[:, 0:1024], AF.Abs)
                    a.activation(arena[2], arena[1], AF.Exp, scale=-1.0)
                    h, _ = p.inc('sA')
                    a.activation(arena[1], arena[2], AF.Ln, bias=c['ONEC']).then_inc(h, 1)
                p.Ae(lsg_a)
                p.wait('vector', 'sA')

                def lsg_v(v, pX=pX):
                    v.scalar_tensor_tensor(arena[2], pX[:, 0:1024], 0.0, arena[1],
                                           ALU.min, ALU.subtract)
                    h, _ = p.inc('sV')
                    v.tensor_scalar(GKT, arena[2], 1.0 / 16.0,
                                    None, ALU.mult).then_inc(h, 1)
                p.Ve(lsg_v)
                p.wait('tensor', 'sV')
                p.wait('tensor', 'sA')

                if K_GCUT < 3:
                    continue
                def emm(t_, pX=pX, gkc=gkc):
                    ins = None
                    for g in range(2):
                        t_.matmul(pX[:, g * 512:(g + 1) * 512], c['LSN'],
                                  gkc[:, g * 512:(g + 1) * 512],
                                  start=True, stop=True)
                        ins = t_.matmul(pX[0:1, 1024 + g * 512:1024 + (g + 1) * 512],
                                        c['ONESCOLF'],
                                        gkc[:, g * 512:(g + 1) * 512],
                                        start=True, stop=True)
                    h, _ = p.inc('sP')
                    ins.then_inc(h, 1)
                p.Te(emm)
                p.wait('scalar', 'sP')

                if K_GCUT < 4:
                    continue
                def exps(a, pX=pX):
                    a.activation(arena[0], pX[:, 0:1024], AF.Exp, scale=-1.0)
                    a.activation(arena[1], pX[:, 0:1024], AF.Exp)
                    h, _ = p.inc('sA')
                    a.activation(X1[0:1, 0:D], pX[0:1, 1024:2048],
                                 AF.Exp).then_inc(h, 1)
                p.Ae(exps)
                p.wait('tensor', 'sA')

                if K_GCUT < 5:
                    continue
                def pbc(t_, pX=pX):
                    ins = None
                    for g in range(2):
                        ins = t_.matmul(pX[0:64, 1024 + g * 512:1024 + (g + 1) * 512],
                                        c['ONESROW'][0:1, 0:64],
                                        X1[0:1, g * 512:(g + 1) * 512],
                                        start=True, stop=True)
                    h, _ = p.inc('sP')
                    ins.then_inc(h, 1)
                p.Te(pbc)
                p.wait('vector', 'sA')
                p.wait('vector', 'sP')

                if K_GCUT < 6:
                    continue
                def vb(v, w=w, pX=pX):
                    v.tensor_tensor(VBAR, VTM[:, w * D:(w + 1) * D], arena[0], ALU.mult)
                    h, _ = p.inc('sV')
                    v.tensor_tensor(SPRE, SST, pX[0:64, 1024:2048],
                                    ALU.mult).then_inc(h, 1)
                p.Ve(vb)
                p.wait('scalar', 'sV')

                if K_GCUT < 7:
                    continue
                p.wait('scalar', 'sP')

                def sprecast(a):
                    sp3 = SPRE.rearrange("p (h x) -> p h x", x=64)
                    a.activation(SPREB[0:64, :].rearrange("p (h x) -> p h x", x=64),
                                 sp3[:, 0:NH0:2, :], AF.Identity)
                    h, _ = p.inc('sA')
                    a.activation(SPREB[64:128, :].rearrange("p (h x) -> p h x", x=64),
                                 sp3[:, 1:NH0:2, :], AF.Identity).then_inc(h, 1)
                p.Ae(sprecast)
                p.wait('tensor', 'sA')

                p.guard(pYn)

                if K_GCUT < 8:
                    continue
                def atme(t_, pY=pY, w=w):
                    ins = None
                    for h_ in range(0, NH0, 2):
                        col = (h_ // 2) * T + w * 128
                        ins = t_.matmul(pY[:, h_ * 128:(h_ + 1) * 128],
                                        KFM[0:64, col:col + 128],
                                        QFM[0:64, col:col + 128],
                                        start=True, stop=True)
                    hh, _ = p.inc('sP')
                    ins.then_inc(hh, 1)
                p.Te(atme)
                p.wait('tensor', 'sV')

                def kvmm(t_, pX=pX, w=w):
                    ins = None
                    for h_ in range(NH0):
                        ins = t_.matmul(pX[0:64, 1024 + h_ * 64:1024 + (h_ + 1) * 64],
                                        KTM[:, w * D + h_ * 64:w * D + (h_ + 1) * 64],
                                        VBAR[:, h_ * 64:(h_ + 1) * 64],
                                        start=True, stop=True)
                    hh, _ = p.inc('sP')
                    ins.then_inc(hh, 1)
                p.Te(kvmm)

                def atmo(t_, pY=pY, w=w):
                    ins = None
                    for h_ in range(1, NH0, 2):
                        col = (h_ // 2) * T + w * 128
                        ins = t_.matmul(pY[:, h_ * 128:(h_ + 1) * 128],
                                        KFM[64:128, col:col + 128],
                                        QFM[64:128, col:col + 128],
                                        start=True, stop=True)
                    hh, _ = p.inc('sP')
                    ins.then_inc(hh, 1)
                p.Te(atmo)
                p.wait('vector', 'sP')

                def atmev(v, pY=pY):
                    src = pY.rearrange("q (h t) -> q h t", t=128)
                    msk = c['CAUS'].unsqueeze(1).broadcast_to([128, NH0, 128])
                    dst = ATM.rearrange("q (h t) -> q h t", t=128)
                    h, _ = p.inc('sV')
                    v.tensor_tensor(dst, src, msk, ALU.mult).then_inc(h, 1)
                p.Ve(atmev)
                p.mark(pYn, 'sV')
                p.wait('tensor', 'sV')

                def omm(t_, pX=pX, w=w):
                    ins = None
                    for h_ in range(NH0):
                        base = 64 * (h_ % 2)
                        col = (h_ // 2) * T + w * 128
                        t_.matmul(pX[:, h_ * 64:(h_ + 1) * 64],
                                  ATM[:, h_ * 128:(h_ + 1) * 128],
                                  VBAR[:, h_ * 64:(h_ + 1) * 64],
                                  start=True, stop=False)
                        ins = t_.matmul(pX[:, h_ * 64:(h_ + 1) * 64],
                                        QFM[base:base + 64, col:col + 128],
                                        SPREB[base:base + 64,
                                              (h_ // 2) * 64:(h_ // 2) * 64 + 64],
                                        start=False, stop=True)
                    hh, _ = p.inc('sP')
                    ins.then_inc(hh, 1)
                p.Te(omm)
                p.wait('vector', 'sP')

                def omul(v, pX=pX):
                    v.tensor_tensor(arena[2], pX[:, 0:1024], arena[1], ALU.mult)
                    h, _ = p.inc('sV')
                    v.tensor_tensor(SST, SPRE, pX[0:64, 1024:2048],
                                    ALU.add).then_inc(h, 1)
                p.Ve(omul)
                p.mark(pXn, 'sV')

                if K_GCUT < 12:
                    continue
                stt = ln_stats_cols(p, arena[2], COLS, 16, arena[0], c['EPSC'])
                p.wait('scalar', 'sV')

                def olna(a, stt=stt):
                    h, _ = p.inc('sA')
                    a.activation(arena[0], arena[2], AF.Identity, scale=stt['r'],
                                 bias=stt['negmr']).then_inc(h, 1)
                p.Ae(olna)
                p.wait('vector', 'sA')

                if K_GCUT < 13:
                    continue
                def gmul(v, w=w):
                    v.tensor_tensor(arena[1], GATE[:, w * D:(w + 1) * D], GBC, ALU.mult)
                    v.tensor_tensor(arena[2], arena[0], arena[1], ALU.mult)
                    v.tensor_tensor(arena[1], GATE[:, w * D:(w + 1) * D], BBC, ALU.mult)
                    h, _ = p.inc('sV')
                    v.tensor_tensor(RCB, arena[2], arena[1], ALU.add).then_inc(h, 1)
                p.Ve(gmul)
                p.wait('tensor', 'sV')
                p.guard(pYn)

                if K_GCUT < 14:
                    continue
                def rtr(t_, pYb=pYb):
                    ins = None
                    for f in range(NCH):
                        ins = t_.transpose(pYb[:, f * 128:(f + 1) * 128],
                                           RCB[:, f * 128:(f + 1) * 128], c['IDBF'])
                    h, _ = p.inc('sP')
                    ins.then_inc(h, 1)
                p.Te(rtr)
                p.wait('scalar', 'sP')

                def rev(a, pYb=pYb, w=w):
                    d3 = RFM.rearrange("q (f x) -> q f x", x=T)[
                        :, :, w * 128:w * 128 + 128]
                    s3 = pYb[:, 0:1024].rearrange("q (f x) -> q f x", x=128)
                    h, _ = p.inc('sA')
                    a.activation(d3, s3, AF.Identity).then_inc(h, 1)
                p.Ae(rev)
                p.mark(pYn, 'sA')

        if K_CUT >= 10:
            p.guard('B')

            def poolmm(t_):
                ins = None
                for k in range(NCH):
                    ins = t_.matmul(B0[0:64, 0:T], c['FOLD'], XNF[:, k * T:(k + 1) * T],
                                    start=(k == 0), stop=(k == NCH - 1))
                h, _ = p.inc('sP')
                ins.then_inc(h, 1)
            p.Te(poolmm)
            p.wait('scalar', 'sP')
            PLB = PL1.bitcast(BF16)[0:64, 0:T]

            def poolev(a):
                h, _ = p.inc('sA')
                a.activation(PLB, B0[0:64, 0:T], AF.Identity).then_inc(h, 1)
            p.Ae(poolev)
            p.mark('B', 'sA')

            p.guard('B')
            p.wait('tensor', 'sV')

            def x1mm(t_):
                ins = None
                for w in range(2):
                    for g in range(2):
                        sl = B0[:, w * D + g * 512:w * D + (g + 1) * 512]
                        for k in range(NCH):
                            t_.matmul(sl, RFM[:, k * T + w * 128:k * T + w * 128 + 128],
                                      WO[:, k * D + g * 512:k * D + (g + 1) * 512],
                                      start=(k == 0), stop=False)
                        for k in range(NCH):
                            t_.matmul(sl, XNF[:, k * T + w * 128:k * T + w * 128 + 128],
                                      WF[:, k * D + g * 512:k * D + (g + 1) * 512],
                                      start=False, stop=False)
                        ins = t_.matmul(sl, PLB[:, w * 128:(w + 1) * 128],
                                        DISHA[:, g * 512:(g + 1) * 512],
                                        start=False, stop=True)
                h, _ = p.inc('sP')
                ins.then_inc(h, 1)
            p.Te(x1mm)
            p.wait('vector', 'sP')

            def x1add(v, X0t=X0t):
                v.tensor_tensor(X1[:, 0:1024], X0t[:, 0:1024], B0[:, 0:1024], ALU.add)
                h, _ = p.inc('sV')
                v.tensor_tensor(X1[:, 1024:2048], X0t[:, 1024:2048],
                                B0[:, 1024:2048], ALU.add).then_inc(h, 1)
            p.Ve(x1add)
            x1add_cnt[t] = p.val('sV')
            p.mark('B', 'sV')

            p.wait('sync', 'sV', x1add_cnt[t])

            def x1st(y, t=t):
                dst = x1_d[t * T:(t + 1) * T, :].rearrange("(w q) f -> q w f", q=128)
                h, _ = p.inc('sS', 16)
                y.dma_start(dst, X1.rearrange("q (w f) -> q w f", w=2)).then_inc(h, 16)
            p.Sy(x1st)

        if K_CUT >= 11:
            for w in range(2):
                x1w = X1[:, w * 1024:(w + 1) * 1024]
                st1 = ln_stats_cols(p, x1w, COLS, 24, arena[0], c['EPSC'])
                p.wait('scalar', 'sV')

                def xn1a(a, x1w=x1w, st1=st1):
                    h, _ = p.inc('sA')
                    a.activation(XN1, x1w, AF.Identity, scale=st1['r'],
                                 bias=st1['negmr']).then_inc(h, 1)
                p.Ae(xn1a)

                p.wait('vector', 'sA')

                def rhov(v, st1=st1):
                    v.tensor_scalar(COLS[:, 28:29], st1['v'], st1['r'], None, ALU.mult)
                    h, _ = p.inc('sV')
                    v.tensor_scalar(COLS[:, 28:29], COLS[:, 28:29], st1['r'],
                                    None, ALU.mult).then_inc(h, 1)
                p.Ve(rhov)
                p.wait('scalar', 'sV')

                def rhoa(a):
                    h, _ = p.inc('sA')
                    a.activation(COLS[:, 29:30], COLS[:, 28:29], AF.Ln,
                                 bias=c['EPSC']).then_inc(h, 1)
                p.Ae(rhoa)
                p.wait('scalar', 'sA')

                def rhoa2(a):
                    h, _ = p.inc('sA')
                    a.activation(COLS[:, 29:30], COLS[:, 29:30], AF.Exp,
                                 scale=-0.5).then_inc(h, 1)
                p.Ae(rhoa2)
                p.wait('vector', 'sA')

                def potv(v):
                    v.tensor_tensor(arena[1], XN1, GWBC[:, 0:D], ALU.mult)
                    v.reduce_sum(COLS[:, 30:31], arena[1], AX.X)
                    v.tensor_tensor(arena[2], XN1, GWBC[:, D:2 * D], ALU.mult)
                    h, _ = p.inc('sV')
                    v.reduce_sum(COLS[:, 31:32], arena[2], AX.X).then_inc(h, 1)
                p.Ve(potv)
                p.wait('vector', 'sV')

                def potv2(v, w=w, t=t):
                    v.tensor_scalar(COLS[:, 30:31], COLS[:, 30:31], COLS[:, 29:30],
                                    None, ALU.mult)
                    v.tensor_scalar(COLS[:, 31:32], COLS[:, 31:32], COLS[:, 29:30],
                                    None, ALU.mult)
                    v.tensor_scalar(VT[:, 2 * t + w:2 * t + w + 1], COLS[:, 30:31],
                                    BWC[:, 0:1], None, ALU.add)
                    h, _ = p.inc('sV')
                    v.tensor_scalar(COLS[:, 31:32], COLS[:, 31:32], BWC[:, 1:2],
                                    None, ALU.add).then_inc(h, 1)
                p.Ve(potv2)
                p.wait('scalar', 'sV')

                def gma(a):
                    h, _ = p.inc('sA')
                    a.activation(COLS[:, 28:29], COLS[:, 31:32],
                                 AF.Abs).then_inc(h, 1)
                p.Ae(gma)
                p.wait('scalar', 'sA')

                def gma2(a):
                    h, _ = p.inc('sA')
                    a.activation(COLS[:, 28:29], COLS[:, 28:29], AF.Exp,
                                 scale=-1.0).then_inc(h, 1)
                p.Ae(gma2)
                p.wait('scalar', 'sA')

                def gma3(a):
                    h, _ = p.inc('sA')
                    a.activation(COLS[:, 28:29], COLS[:, 28:29], AF.Ln,
                                 bias=c['ONEC']).then_inc(h, 1)
                p.Ae(gma3)
                p.wait('vector', 'sA')

                def gmv(v, w=w, t=t):
                    h, _ = p.inc('sV')
                    v.tensor_scalar(COLS[:, 29:30], COLS[:, 31:32], 0.0, None,
                                    ALU.max).then_inc(h, 1)
                p.Ve(gmv)
                p.wait('vector', 'sV')

                def gmv2(v, w=w, t=t):
                    h, _ = p.inc('sV')
                    v.tensor_scalar(GT[:, 2 * t + w:2 * t + w + 1], COLS[:, 29:30],
                                    COLS[:, 28:29], 0.1, ALU.add,
                                    ALU.add).then_inc(h, 1)
                p.Ve(gmv2)







def emit_scan(p, c, io, VT, GT, keep):
    SRE = p.sb('SRE', [128, 264], F32)
    SIM = p.sb('SIM', [128, 264], F32)
    SCL = p.sb('SCL', [128, 8], F32)
    DRE = p.sb('DRE', [128, 128], F32)
    DIM = p.sb('DIM', [128, 128], F32)
    T12 = p.sb('T12', [128, 8], F32)
    TRS = p.sb('TRS', [8, 128], F32)
    TRR = p.sb('TRR', [1, 1024], F32)
    STR = p.sb('STR', [1, 64], F32)
    ENT = p.sb('ENT', [1, 520], F32)
    P0C = p.sb('P0C', [128, 4], F32)
    ANM = p.sb('ANM', [128, 4 * 128], F32)
    AAB = p.sb('AAB', [128, 2 * 128], F32)
    GRE = p.sb('GRE', [64, 128], F32)
    GIM = p.sb('GIM', [64, 128], F32)
    EHC = p.sb('EHC', [128, 64], F32)
    EHNE = p.sb('EHNE', [128, 64], F32)
    EHP = p.sb('EHP', [128, 64], F32)
    DCC = p.sb('DCC', [64, 64], F32)
    J64T = p.sb('J64T', [128, 64], F32)
    LSDT = p.sb('LSDT', [128, 128], F32)
    DECR = p.sb('DECR', [1, 64], F32)
    keep.update(EHC=EHC, EHNE=EHNE, EHP=EHP, DCC=DCC, SRE=SRE, SIM=SIM,
                TRR=TRR, ENT=ENT, SCL=SCL, AAB=AAB, P0C=P0C, DRE=DRE, DIM=DIM)
    A0, B0 = p.psA, p.psB
    p.sem('sDR')

    def cns(g):
        g.memset(J64T, 0.0)
        g.affine_select(out=J64T[64:128, :], in_=J64T[64:128, :],
                        compare_op=ALU.not_equal, fill=1.0, base=-63,
                        pattern=[[1, 64]], channel_multiplier=1)
        g.memset(LSDT, DTC)
        g.affine_select(out=LSDT, in_=LSDT, compare_op=ALU.is_gt,
                        fill=0.0, base=0, pattern=[[-1, 128]], channel_multiplier=1)
        g.memset(SRE[:, 0:1], 0.0)
        g.memset(SRE[:, 1:2], 1.0)
        g.memset(SRE[:, 2:3], 1.0)
        g.memset(SRE[:, 3:4], 0.0)
        g.memset(SIM[:, 0:4], 0.0)
        g.memset(ENT, 0.0)
        g.memset(ENT[0:1, 0:1], 1.0)
        h, _ = p.inc('sG')
        g.memset(ENT[0:1, 128:129], 1.0).then_inc(h, 1)
    p.Ge(cns)
    for e in ('tensor', 'vector', 'scalar'):
        p.wait(e, 'sG')

    p.guard('A')

    def dmm(t_):
        t_.transpose(A0[0:64, 0:128], VT, c['IDF32'])
        t_.transpose(A0[0:64, 256:384], GT, c['IDF32'])
        t_.matmul(A0[:, 512:576], LSDT, GT, start=True, stop=True)
        h, _ = p.inc('sP')
        t_.matmul(A0[0:64, 384:512], VT, c['IDF32'], start=True,
                  stop=True).then_inc(h, 1)
    p.Te(dmm)
    p.wait('vector', 'sP')
    p.wait('scalar', 'sP')

    def dev1(v):
        v.tensor_scalar(DRE[0:64, :], A0[0:64, 0:128], 2.0, None, ALU.add)
        v.tensor_scalar(DIM[0:64, :], A0[0:64, 256:384], -1.0, None, ALU.mult)
        v.tensor_scalar(DRE[64:128, :], A0[0:64, 127::-1], 2.0, None, ALU.add)
        h, _ = p.inc('sV')
        v.tensor_scalar(DIM[64:128, :], A0[0:64, 383:255:-1], -1.0, None,
                        ALU.mult).then_inc(h, 1)
    p.Ve(dev1)
    p.mark('A', 'sV')

    def ehe(a):
        a.activation(EHC, A0[:, 512:576], AF.Identity)
        a.activation(EHNE, A0[:, 512:576], AF.Exp, scale=-1.0)
        h, _ = p.inc('sA')
        a.activation(EHP, A0[:, 512:576], AF.Exp).then_inc(h, 1)
    p.Ae(ehe)
    p.mark('A', 'sA')
    p.wait('vector', 'sA')

    def ehv(v):
        h, _ = p.inc('sV')
        v.tensor_scalar(EHNE, EHNE, ETA, None, ALU.mult).then_inc(h, 1)
    p.Ve(ehv)
    p.wait('vector', 'sV')

    def ehv1(v):
        h, _ = p.inc('sV')
        v.tensor_scalar(DECR, GT[0:1, :], DTC, None, ALU.mult).then_inc(h, 1)
    p.Ve(ehv1)
    p.wait('vector', 'sV')

    def ehv2(v):
        h, _ = p.inc('sV')
        v.tensor_tensor(DECR, EHC[0:1, :], DECR, ALU.add).then_inc(h, 1)
    p.Ve(ehv2)
    p.wait('scalar', 'sV')

    def ehe2(a):
        h, _ = p.inc('sA')
        a.activation(DECR, DECR, AF.Exp, scale=-1.0).then_inc(h, 1)
    p.Ae(ehe2)
    p.wait('tensor', 'sA')
    p.guard('B')

    def dccm(t_):
        h, _ = p.inc('sP')
        t_.matmul(B0[0:64, 0:64], c['ONESROW'][0:1, 0:64], DECR,
                  start=True, stop=True).then_inc(h, 1)
    p.Te(dccm)
    p.wait('scalar', 'sP')

    def dcce(a):
        h, _ = p.inc('sA')
        a.activation(DCC, B0[0:64, 0:64], AF.Identity).then_inc(h, 1)
    p.Ae(dcce)
    p.mark('B', 'sA')

    segcnt = 0
    for j in range(128):
        Pre = SRE[:, 2 + 2 * j:4 + 2 * j]
        Qre = SRE[:, 2 * j:2 + 2 * j]
        Pim = SIM[:, 2 + 2 * j:4 + 2 * j]
        Qim = SIM[:, 2 * j:2 + 2 * j]
        Nre = SRE[:, 4 + 2 * j:6 + 2 * j]
        Nim = SIM[:, 4 + 2 * j:6 + 2 * j]
        drj = DRE[:, j:j + 1]
        dij = DIM[:, j:j + 1]

        def stepa(v, Pre=Pre, Pim=Pim, drj=drj, dij=dij):
            v.tensor_scalar(T12[:, 0:2], Pre, drj, None, ALU.mult)
            v.tensor_scalar(T12[:, 2:4], Pim, dij, None, ALU.mult)
            v.tensor_scalar(T12[:, 4:6], Pim, drj, None, ALU.mult)
            h, _ = p.inc('sV')
            v.tensor_scalar(T12[:, 6:8], Pre, dij, None, ALU.mult).then_inc(h, 1)
        p.Ve(stepa)
        p.wait('vector', 'sV')

        def stepb(v):
            v.tensor_tensor(T12[:, 0:2], T12[:, 0:2], T12[:, 2:4], ALU.subtract)
            h, _ = p.inc('sV')
            v.tensor_tensor(T12[:, 4:6], T12[:, 4:6], T12[:, 6:8],
                            ALU.add).then_inc(h, 1)
        p.Ve(stepb)
        p.wait('vector', 'sV')

        def stepc(v, Qre=Qre, Qim=Qim, Nre=Nre, Nim=Nim):
            v.tensor_tensor(Nre, T12[:, 0:2], Qre, ALU.subtract)
            h, _ = p.inc('sV')
            v.tensor_tensor(Nim, T12[:, 4:6], Qim, ALU.subtract).then_inc(h, 1)
        p.Ve(stepc)
        p.wait('vector', 'sV')

        if j % 16 == 15 and j < 127:
            seg = segcnt
            segcnt += 1

            def rn1(v, j=j):
                v.reduce_max(T12[:, 0:1], SRE[:, 4 + 2 * j:5 + 2 * j], AX.X,
                             apply_absolute_value=True)
                h, _ = p.inc('sV')
                v.reduce_max(T12[:, 1:2], SIM[:, 4 + 2 * j:5 + 2 * j], AX.X,
                             apply_absolute_value=True).then_inc(h, 1)
            p.Ve(rn1)
            p.wait('vector', 'sV')

            def rn2(v, seg=seg):
                h, _ = p.inc('sV')
                v.tensor_scalar(SCL[:, seg:seg + 1], T12[:, 0:1], T12[:, 1:2],
                                1e-30, ALU.add, ALU.add).then_inc(h, 1)
            p.Ve(rn2)
            p.wait('vector', 'sV')

            def rn3(v, seg=seg):
                h, _ = p.inc('sV')
                v.reciprocal(T12[:, 2:3], SCL[:, seg:seg + 1]).then_inc(h, 1)
            p.Ve(rn3)
            p.wait('vector', 'sV')

            def rn4(v, j=j):
                rc = T12[:, 2:3]
                v.tensor_scalar(SRE[:, 2 + 2 * j:6 + 2 * j],
                                SRE[:, 2 + 2 * j:6 + 2 * j], rc, None, ALU.mult)
                h, _ = p.inc('sV')
                v.tensor_scalar(SIM[:, 2 + 2 * j:6 + 2 * j],
                                SIM[:, 2 + 2 * j:6 + 2 * j], rc, None,
                                ALU.mult).then_inc(h, 1)
            p.Ve(rn4)
            p.wait('vector', 'sV')

    def crp(v):
        v.tensor_copy(T12[:, 0:2], SRE[:, 258:260])
        v.tensor_copy(T12[:, 2:4], SRE[:, 256:258])
        v.tensor_copy(T12[:, 4:6], SIM[:, 258:260])
        h, _ = p.inc('sV')
        v.tensor_copy(T12[:, 6:8], SIM[:, 256:258]).then_inc(h, 1)
    p.Ve(crp)
    p.wait('tensor', 'sV')
    p.guard('A')

    def crt(t_):
        h, _ = p.inc('sP')
        t_.transpose(A0[0:8, 0:128], T12, c['IDF32']).then_inc(h, 1)
    p.Te(crt)
    p.wait('scalar', 'sP')

    def cre(a):
        h, _ = p.inc('sA')
        a.activation(TRS, A0[0:8, 0:128], AF.Identity).then_inc(h, 1)
    p.Ae(cre)
    p.mark('A', 'sA')
    p.wait('sync', 'sA')

    def crd(y):
        h = p.sem('sDR')
        y.dma_start(TRR[0:1, :].rearrange("o (k l) -> o k l", k=8),
                    TRS).then_inc(h, 16)
    p.Sy(crd)
    p.cnt['sDR'] += 16
    p.wait('vector', 'sDR', p.val('sDR'))

    def sti(v):
        v.memset(STR, 0.0)
        h, _ = p.inc('sV')
        v.memset(STR[0:1, 0:2], 1.0).then_inc(h, 1)
    p.Ve(sti)
    p.wait('vector', 'sV')

    for cc in range(64):
        stfw = 127 - 2 * cc

        def tvk(k, cc=cc, stfw=stfw):
            return TRR[0:1, k * 128 + cc::stfw][:, 0:2]

        def cs1(v, cc=cc, stfw=stfw):
            tv = lambda k: TRR[0:1, k * 128 + cc::stfw][:, 0:2]
            pr, pi = STR[0:1, 0:2], STR[0:1, 2:4]
            qr, qi = STR[0:1, 4:6], STR[0:1, 6:8]
            prods = [(0, pr), (4, pi), (1, qr), (5, qi),
                     (0, pi), (4, pr), (1, qi), (5, qr),
                     (2, pr), (6, pi), (3, qr), (7, qi),
                     (2, pi), (6, pr), (3, qi), (7, qr)]
            ins = None
            for i, (k, sop) in enumerate(prods):
                ins = v.tensor_tensor(STR[0:1, 16 + 2 * i:18 + 2 * i],
                                      tv(k), sop, ALU.mult)
            h, _ = p.inc('sV')
            ins.then_inc(h, 1)
        p.Ve(cs1)
        p.wait('vector', 'sV')

        def cs2(v):
            ops = [(0, 1, ALU.subtract), (2, 3, ALU.subtract),
                   (4, 5, ALU.add), (6, 7, ALU.add),
                   (8, 9, ALU.subtract), (10, 11, ALU.subtract),
                   (12, 13, ALU.add), (14, 15, ALU.add)]
            ins = None
            for i, (a, b, op) in enumerate(ops):
                ins = v.tensor_tensor(STR[0:1, 48 + 2 * i:50 + 2 * i],
                                      STR[0:1, 16 + 2 * a:18 + 2 * a],
                                      STR[0:1, 16 + 2 * b:18 + 2 * b], op)
            h, _ = p.inc('sV')
            ins.then_inc(h, 1)
        p.Ve(cs2)
        p.wait('vector', 'sV')

        def cs3(v):
            ins = None
            for i in range(4):
                ins = v.tensor_tensor(STR[0:1, 2 * i:2 * i + 2],
                                      STR[0:1, 48 + 4 * i:50 + 4 * i],
                                      STR[0:1, 50 + 4 * i:52 + 4 * i], ALU.add)
            h, _ = p.inc('sV')
            ins.then_inc(h, 1)
        p.Ve(cs3)
        p.wait('vector', 'sV')

        def crn(v):
            v.tensor_tensor(STR[0:1, 16:18], STR[0:1, 0:2], STR[0:1, 0:2], ALU.mult)
            v.tensor_tensor(STR[0:1, 18:20], STR[0:1, 2:4], STR[0:1, 2:4], ALU.mult)
            v.tensor_tensor(STR[0:1, 20:22], STR[0:1, 4:6], STR[0:1, 4:6], ALU.mult)
            h, _ = p.inc('sV')
            v.tensor_tensor(STR[0:1, 22:24], STR[0:1, 6:8], STR[0:1, 6:8],
                            ALU.mult).then_inc(h, 1)
        p.Ve(crn)
        p.wait('vector', 'sV')

        def crnb(v):
            v.tensor_tensor(STR[0:1, 16:18], STR[0:1, 16:18], STR[0:1, 18:20], ALU.add)
            h, _ = p.inc('sV')
            v.tensor_tensor(STR[0:1, 20:22], STR[0:1, 20:22], STR[0:1, 22:24],
                            ALU.add).then_inc(h, 1)
        p.Ve(crnb)
        p.wait('vector', 'sV')

        def crnc(v):
            h, _ = p.inc('sV')
            v.scalar_tensor_tensor(STR[0:1, 16:18], STR[0:1, 16:18], 1e-30,
                                   STR[0:1, 20:22], ALU.add,
                                   ALU.add).then_inc(h, 1)
        p.Ve(crnc)
        p.wait('vector', 'sV')

        def crnd(v):
            h, _ = p.inc('sV')
            v.reciprocal(STR[0:1, 18:20], STR[0:1, 16:18]).then_inc(h, 1)
        p.Ve(crnd)
        p.wait('vector', 'sV')

        def crn2(v):
            v.tensor_tensor(STR[0:1, 0:2], STR[0:1, 0:2], STR[0:1, 18:20], ALU.mult)
            v.tensor_tensor(STR[0:1, 2:4], STR[0:1, 2:4], STR[0:1, 18:20], ALU.mult)
            v.tensor_tensor(STR[0:1, 4:6], STR[0:1, 4:6], STR[0:1, 18:20], ALU.mult)
            h, _ = p.inc('sV')
            v.tensor_tensor(STR[0:1, 6:8], STR[0:1, 6:8], STR[0:1, 18:20],
                            ALU.mult).then_inc(h, 1)
        p.Ve(crn2)
        p.wait('vector', 'sV')

        if cc < 63:
            def entw(v, cc=cc):
                for k, sc in ((0, 0), (1, 2), (2, 4), (3, 6)):
                    fc = k * 130 + cc + 1
                    bc = k * 130 + 65 + 62 - cc
                    v.tensor_copy(ENT[0:1, fc:fc + 1], STR[0:1, sc:sc + 1])
                    ins = v.tensor_copy(ENT[0:1, bc:bc + 1],
                                        STR[0:1, sc + 1:sc + 2])
                h, _ = p.inc('sV')
                ins.then_inc(h, 1)
            p.Ve(entw)
            p.wait('vector', 'sV')

    p.wait('tensor', 'sV')
    p.guard('A')

    def p0t(t_):
        ins = None
        for k in range(4):
            t_.transpose(A0[0:64, 1024 + k * 64:1024 + k * 64 + 1],
                         ENT[0:1, k * 130:k * 130 + 64], c['IDF32'][0:1, 0:1])
            ins = t_.transpose(A0[0:64, 1280 + k * 64:1280 + k * 64 + 1],
                               ENT[0:1, k * 130 + 65:k * 130 + 129],
                               c['IDF32'][0:1, 0:1])
        h, _ = p.inc('sP')
        ins.then_inc(h, 1)
    p.Te(p0t)
    p.wait('scalar', 'sP')

    def p0e(a):
        ins = None
        for k in range(4):
            a.activation(P0C[0:64, k:k + 1],
                         A0[0:64, 1024 + k * 64:1024 + k * 64 + 1], AF.Identity)
            ins = a.activation(P0C[64:128, k:k + 1],
                               A0[0:64, 1280 + k * 64:1280 + k * 64 + 1], AF.Identity)
        h, _ = p.inc('sA')
        ins.then_inc(h, 1)
    p.Ae(p0e)
    p.mark('A', 'sA')
    p.wait('vector', 'sA')

    def negp(v):
        h, _ = p.inc('sV')
        v.tensor_scalar(T12[:, 0:1], P0C[:, 1:2], -1.0, None, ALU.mult).then_inc(h, 1)
    p.Ve(negp)
    p.wait('vector', 'sV')

    Ure = SRE[:, 4::2][:, 0:128]
    Wre = SRE[:, 5::2][:, 0:128]
    Uim = SIM[:, 4::2][:, 0:128]
    Wim = SIM[:, 5::2][:, 0:128]
    UreD = SRE[:, 2::2][:, 0:128]
    WreD = SRE[:, 3::2][:, 0:128]
    UimD = SIM[:, 2::2][:, 0:128]
    WimD = SIM[:, 3::2][:, 0:128]
    NRE, NIM = ANM[:, 0:128], ANM[:, 128:256]
    DREn, DIMn = ANM[:, 256:384], ANM[:, 384:512]
    SC1, SC2 = AAB[:, 0:128], AAB[:, 128:256]
    APS = p.sb('APS', [128, 1024], F32)

    combos = [
        (NRE, (Ure, P0C[:, 0:1]), (Uim, T12[:, 0:1]), (Wre, P0C[:, 2:3]),
         (Wim, P0C[:, 3:4]), True),
        (NIM, (Uim, P0C[:, 0:1]), (Ure, P0C[:, 1:2]), (Wim, P0C[:, 2:3]),
         (Wre, P0C[:, 3:4]), False),
        (DREn, (UreD, P0C[:, 0:1]), (UimD, T12[:, 0:1]), (WreD, P0C[:, 2:3]),
         (WimD, P0C[:, 3:4]), True),
        (DIMn, (UimD, P0C[:, 0:1]), (UreD, P0C[:, 1:2]), (WimD, P0C[:, 2:3]),
         (WreD, P0C[:, 3:4]), False),
    ]
    for rnd in (0, 1):
        pair = combos[2 * rnd:2 * rnd + 2]

        def apa(v, pair=pair):
            ins = None
            for ci, (_, t1, t2, t3, t4, _) in enumerate(pair):
                for ti, (a, s) in enumerate((t1, t2, t3, t4)):
                    sl = APS[:, (ci * 4 + ti) * 128:(ci * 4 + ti) * 128 + 128]
                    ins = v.tensor_scalar(sl, a, s, None, ALU.mult)
            h, _ = p.inc('sV')
            ins.then_inc(h, 1)
        p.Ve(apa)
        p.wait('vector', 'sV')

        def apb(v, pair=pair):
            ins = None
            for ci, (_, _, _, _, _, neg4) in enumerate(pair):
                b = ci * 4 * 128
                v.tensor_tensor(APS[:, b:b + 128], APS[:, b:b + 128],
                                APS[:, b + 128:b + 256], ALU.add)
                ins = v.tensor_tensor(
                    APS[:, b + 256:b + 384], APS[:, b + 256:b + 384],
                    APS[:, b + 384:b + 512],
                    ALU.subtract if neg4 else ALU.add)
            h, _ = p.inc('sV')
            ins.then_inc(h, 1)
        p.Ve(apb)
        p.wait('vector', 'sV')

        def apc(v, pair=pair):
            ins = None
            for ci, (out, _, _, _, _, _) in enumerate(pair):
                b = ci * 4 * 128
                ins = v.tensor_tensor(out, APS[:, b:b + 128],
                                      APS[:, b + 256:b + 384], ALU.add)
            h, _ = p.inc('sV')
            ins.then_inc(h, 1)
        p.Ve(apc)
        p.wait('vector', 'sV')

    def ap2a(v):
        v.tensor_tensor(SC1, DREn, DREn, ALU.mult)
        h, _ = p.inc('sV')
        v.tensor_tensor(SC2, DIMn, DIMn, ALU.mult).then_inc(h, 1)
    p.Ve(ap2a)
    p.wait('vector', 'sV')

    def ap2b(v):
        h, _ = p.inc('sV')
        v.tensor_tensor(SC1, SC1, SC2, ALU.add).then_inc(h, 1)
    p.Ve(ap2b)
    p.wait('vector', 'sV')

    def ap2c(v):
        h, _ = p.inc('sV')
        v.reciprocal(SC2, SC1).then_inc(h, 1)
    p.Ve(ap2c)
    p.wait('vector', 'sV')

    def ap3a(v):
        v.tensor_tensor(SC1, NRE, DREn, ALU.mult)
        v.tensor_tensor(DREn, NIM, DREn, ALU.mult)
        v.tensor_tensor(NRE, NRE, DIMn, ALU.mult)
        h, _ = p.inc('sV')
        v.tensor_tensor(NIM, NIM, DIMn, ALU.mult).then_inc(h, 1)
    p.Ve(ap3a)
    p.wait('vector', 'sV')

    def ap3b(v):
        v.tensor_tensor(SC1, SC1, NIM, ALU.add)
        h, _ = p.inc('sV')
        v.tensor_tensor(DREn, DREn, NRE, ALU.subtract).then_inc(h, 1)
    p.Ve(ap3b)
    p.wait('vector', 'sV')

    def ap3c(v):
        v.tensor_tensor(AAB[:, 0:128], SC1, SC2, ALU.mult)
        h, _ = p.inc('sV')
        v.tensor_tensor(AAB[:, 128:256], DREn, SC2, ALU.mult).then_inc(h, 1)
    p.Ve(ap3c)
    p.wait('vector', 'sV')

    for seg in range(7):
        jcol = 16 * seg + 14

        def segfix(v, jcol=jcol, seg=seg):
            v.tensor_scalar(AAB[:, jcol:jcol + 1], AAB[:, jcol:jcol + 1],
                            SCL[:, seg:seg + 1], None, ALU.mult)
            h, _ = p.inc('sV')
            v.tensor_scalar(AAB[:, 128 + jcol:129 + jcol],
                            AAB[:, 128 + jcol:129 + jcol],
                            SCL[:, seg:seg + 1], None, ALU.mult).then_inc(h, 1)
        p.Ve(segfix)
        p.wait('vector', 'sV')

    p.wait('scalar', 'sV')

    def balc(a):
        a.activation(ANM[0:64, 256:384], AAB[64:128, 127::-1], AF.Identity)
        h, _ = p.inc('sA')
        a.activation(ANM[0:64, 384:512], AAB[64:128, 255:127:-1],
                     AF.Identity).then_inc(h, 1)
    p.Ae(balc)
    p.wait('vector', 'sA')

    def gfin(v):
        v.tensor_tensor(GRE, AAB[0:64, 0:128], ANM[0:64, 256:384], ALU.add)
        h, _ = p.inc('sV')
        v.tensor_tensor(GIM, AAB[0:64, 128:256], ANM[0:64, 384:512],
                        ALU.add).then_inc(h, 1)
    p.Ve(gfin)
    p.wait('vector', 'sV')

    def gfin2(v):
        v.tensor_tensor(GRE, GRE, DRE[0:64, :], ALU.subtract)
        h, _ = p.inc('sV')
        v.tensor_tensor(GIM, GIM, DIM[0:64, :], ALU.subtract).then_inc(h, 1)
    p.Ve(gfin2)
    p.wait('vector', 'sV')

    def gfin2b(v):
        v.tensor_tensor(ANM[0:64, 0:128], GRE, GRE, ALU.mult)
        h, _ = p.inc('sV')
        v.tensor_tensor(ANM[0:64, 128:256], GIM, GIM, ALU.mult).then_inc(h, 1)
    p.Ve(gfin2b)
    p.wait('vector', 'sV')

    def gfin2c(v):
        h, _ = p.inc('sV')
        v.tensor_tensor(ANM[0:64, 0:128], ANM[0:64, 0:128],
                        ANM[0:64, 128:256], ALU.add).then_inc(h, 1)
    p.Ve(gfin2c)
    p.wait('vector', 'sV')

    def gfin2d(v):
        h, _ = p.inc('sV')
        v.reciprocal(ANM[0:64, 128:256], ANM[0:64, 0:128]).then_inc(h, 1)
    p.Ve(gfin2d)
    p.wait('vector', 'sV')

    def gfin4(v):
        v.tensor_tensor(GRE, GRE, ANM[0:64, 128:256], ALU.mult)
        h, _ = p.inc('sV')
        v.scalar_tensor_tensor(GIM, GIM, -1.0, ANM[0:64, 128:256],
                               ALU.mult, ALU.mult).then_inc(h, 1)
    p.Ve(gfin4)
    p.wait('sync', 'sV')

    def gst(y):
        h = p.sem('sDR')
        y.dma_start(io['gre'][:], GRE).then_inc(h, 16)
        y.dma_start(io['gim'][:], GIM).then_inc(h, 16)
    p.Sy(gst)
    p.cnt['sDR'] += 32
    p.wait('sync', 'sDR', p.val('sDR'))




def emit_phaseA_setup(p, c, io):
    s = {}
    s['WQKV'] = p.sb('WQKV', [128, NCH * 1536], BF16)
    s['WOUT1'] = p.sb('WOUT1', [128, 4 * D], BF16)
    s['W8D'] = p.sb('W8D', [128, NCH * D], BF16)
    s['U8'] = p.sb('U8', [128, NCH * D], U8)
    s['BKL'] = p.sb('BKL', [3, 8192], BF16)
    s['WBK3'] = p.sb('WBK3', [3, D], BF16)
    s['B2CB'] = p.sb('B2CB', [128, NCH], BF16)
    s['BQKVR'] = p.sb('BQKVR', [1, 1536], BF16)
    s['BOUTR'] = p.sb('BOUTR', [1, D], BF16)
    s['B8R'] = p.sb('B8R', [1, D], F32)
    s['WSC'] = p.sb('WSC', [128, 1], F32)
    s['G2C'] = p.sb('G2C', [128, NCH], F32)
    s['BQR'] = p.sb('BQR', [1, 1536], F32)
    s['BQC'] = p.sb('BQC', [128, 8], F32)
    s['B8BC'] = p.sb('B8BC', [128, D], F32)
    s['BQVBC'] = p.sb('BQVBC', [128, D], F32)
    s['EHNE2'] = p.sb('EHNE2', [128, 64], F32)
    s['EHP2'] = p.sb('EHP2', [128, 64], F32)
    s['DCC2'] = p.sb('DCC2', [64, 64], F32)
    s['SST1'] = p.sb('SST1', [64, NH1 * 64], F32)
    s['SPRE1'] = p.sb('SPRE1', [64, NH1 * 64], F32)
    s['SPB'] = p.sb('SPB', [128, 4 * 64], BF16)
    A0, B0 = p.psA, p.psB

    p.sem('sWD'); p.sem('sX'); p.sem('sS')

    def wload(g):
        h = p.sem('sWD')
        g.dma_start(s['WQKV'].rearrange("q (c n) -> q c n", c=NCH),
                    io['wqkv'].rearrange("(c q) n -> q c n", q=128)).then_inc(h, 16)
        g.dma_start(s['WOUT1'].rearrange("q (c n) -> q c n", c=4),
                    io['wout1'].rearrange("(c q) n -> q c n", q=128)).then_inc(h, 16)
        g.dma_start(s['BKL'][1:2, :].rearrange("o (k l) -> o k l", k=64),
                    io['gre'][:]).then_inc(h, 16)
        g.dma_start(s['BKL'][2:3, :].rearrange("o (k l) -> o k l", k=64),
                    io['gim'][:]).then_inc(h, 16)
        g.dma_start(s['WBK3'], io['wbk3'][:]).then_inc(h, 16)
        g.dma_start(s['B2CB'], io['b2c'][:]).then_inc(h, 16)
        g.dma_start(s['BQKVR'], io['bqkvr'][:]).then_inc(h, 16)
        g.dma_start(s['BOUTR'], io['boutr'][:]).then_inc(h, 16)
        h2, _ = p.inc('sG')
        g.memset(s['BKL'][0:1, :], 1.0).then_inc(h2, 1)
    p.cnt['sWD'] += 16 * 8
    p.Ge(wload)

    def wload2(y):
        h, _ = p.inc('sX', 16 * 7)
        y.dma_start(s['U8'].rearrange("q (c n) -> q c n", c=NCH),
                    io['w8t'].rearrange("(c q) n -> q c n", q=128)).then_inc(h, 16)
        y.dma_start(s['B8R'], io['bias8r'][:]).then_inc(h, 16)
        y.dma_start(s['WSC'], io['wscalec'][:]).then_inc(h, 16)
        y.dma_start(s['G2C'], io['g2c'][:]).then_inc(h, 16)
        y.dma_start(s['EHNE2'], io['ehd'][:, 64:128]).then_inc(h, 16)
        y.dma_start(s['EHP2'], io['ehd'][:, 128:192]).then_inc(h, 16)
        y.dma_start(s['DCC2'], io['ehd'][0:64, 192:256]).then_inc(h, 16)
    p.Sy(wload2)

    for e in ('tensor', 'vector', 'scalar'):
        p.wait(e, 'sG')
        p.wait(e, 'sWD')
        p.wait(e, 'sX')

    AR = p.sb('ARA', [128, 3 * D], F32)
    s['ARA'] = AR
    XT = p.sb('X1T', [128, 2 * 2048], F32)
    s['X1T'] = XT
    T0f, T1f, T2f = AR[:, 0:D], AR[:, D:2 * D], AR[:, 2 * D:3 * D]
    T0i = AR.bitcast(I32)[:, 0:D]
    T1i = AR.bitcast(I32)[:, D:2 * D]
    T2i = AR.bitcast(I32)[:, 2 * D:3 * D]
    T3f, T4f, T5f = XT[:, 0:D], XT[:, D:2 * D], XT[:, 2 * D:3 * D]
    T3i = XT.bitcast(I32)[:, 0:D]
    for k in range(NCH):
        u8c = s['U8'][:, k * D:(k + 1) * D]

        def dec1(v, u8c=u8c):
            v.tensor_copy(T0i, u8c)
            h, _ = p.inc('sV')
            v.memset(T5f, 0.0).then_inc(h, 1)
        p.Ve(dec1)
        p.wait('vector', 'sV')

        def dec2(v):
            v.tensor_scalar(T1i, T0i, 7, None, ALU.logical_shift_right)
            v.tensor_scalar(T0i, T0i, 127, None, ALU.bitwise_and)
            v.tensor_scalar(T2i, T0i, 3, None, ALU.logical_shift_right)
            v.tensor_scalar(T0i, T0i, 7, None, ALU.bitwise_and)
            v.tensor_scalar_min(T3i, T2i, 1)
            v.scalar_tensor_tensor(T4f, T3i, 8, T0i, ALU.mult, ALU.add)
            v.scalar_tensor_tensor(T0f, T2i, -9, T3i, ALU.add, ALU.subtract)
            v.scalar_tensor_tensor(T5f, T1i, -2.0, T5f, ALU.mult, ALU.add)
            h, _ = p.inc('sV')
            v.tensor_scalar(T5f, T5f, 1.0, None, ALU.add).then_inc(h, 1)
        p.Ve(dec2)
        p.wait('scalar', 'sV')

        def dec3(a):
            h, _ = p.inc('sA')
            a.activation(T2f, T0f, AF.Exp, scale=LN2).then_inc(h, 1)
        p.Ae(dec3)
        p.wait('vector', 'sA')

        def dec4(v, k=k):
            v.tensor_tensor(T4f, T4f, T2f, ALU.mult)
            h, _ = p.inc('sV')
            v.tensor_tensor(s['W8D'][:, k * D:(k + 1) * D], T4f, T5f,
                            ALU.mult).then_inc(h, 1)
        p.Ve(dec4)
        p.wait('vector', 'sV')
    p.wait('tensor', 'sV')

    p.guard('A')

    def bqmm(t_):
        ins = None
        for g in range(3):
            for k in range(NCH):
                t_.matmul(A0[0:1, g * 512:(g + 1) * 512], s['B2CB'][:, k:k + 1],
                          s['WQKV'][:, k * 1536 + g * 512:k * 1536 + (g + 1) * 512],
                          start=(k == 0), stop=False)
            ins = t_.matmul(A0[0:1, g * 512:(g + 1) * 512], c['ONESROWB'][0:1, 0:1],
                            s['BQKVR'][0:1, g * 512:(g + 1) * 512],
                            start=False, stop=True)
        h, _ = p.inc('sP')
        ins.then_inc(h, 1)
    p.Te(bqmm)
    p.wait('scalar', 'sP')

    def bqev(a):
        h, _ = p.inc('sA')
        a.activation(s['BQR'], A0[0:1, 0:1536], AF.Identity).then_inc(h, 1)
    p.Ae(bqev)
    p.mark('A', 'sA')

    def g2f(a):
        ins = None
        for k in range(NCH):
            ins = a.activation(s['WQKV'][:, k * 1536:(k + 1) * 1536],
                               s['WQKV'][:, k * 1536:(k + 1) * 1536],
                               AF.Identity, scale=s['G2C'][:, k:k + 1])
        h, _ = p.inc('sA')
        ins.then_inc(h, 1)
    p.Ae(g2f)
    p.wait('tensor', 'sA')
    p.wait('vector', 'sA')
    p.guard('B')

    def bqt(t_):
        ins = None
        for j in range(8):
            ins = t_.transpose(B0[:, j:j + 1], s['BQR'][0:1, j * 128:(j + 1) * 128],
                               c['IDF32'][0:1, 0:1])
        for g in range(2):
            ins = t_.matmul(B0[:, 512 + g * 512:512 + (g + 1) * 512], c['ONESROW'],
                            s['B8R'][0:1, g * 512:(g + 1) * 512], start=True, stop=True)
        h, _ = p.inc('sP')
        ins.then_inc(h, 1)
    p.Te(bqt)
    p.guard('A', 'tensor')

    def bqt2(t_):
        ins = None
        for g in range(2):
            ins = t_.matmul(A0[:, g * 512:(g + 1) * 512], c['ONESROW'],
                            s['BQR'][0:1, 512 + g * 512:512 + (g + 1) * 512],
                            start=True, stop=True)
        h, _ = p.inc('sP')
        ins.then_inc(h, 1)
    p.Te(bqt2)
    p.wait('scalar', 'sP')

    def bqe2(a):
        a.activation(s['BQC'], B0[:, 0:8], AF.Identity)
        a.activation(s['B8BC'], B0[:, 512:1536], AF.Identity)
        h, _ = p.inc('sA')
        a.activation(s['BQVBC'], A0[:, 0:1024], AF.Identity).then_inc(h, 1)
    p.Ae(bqe2)
    p.mark('B', 'sA')
    p.mark('A', 'sA')
    p.wait('vector', 'sA')

    def sstz(v):
        h, _ = p.inc('sV')
        v.memset(s['SST1'], 0.0).then_inc(h, 1)
    p.Ve(sstz)
    p.wait('tensor', 'sV')
    return s


def emit_phaseA(p, c, io, s):
    A0, B0 = p.psA, p.psB
    A0b, B0b = A0.bitcast(BF16), B0.bitcast(BF16)
    XT = s['X1T']
    AR = s['ARA']
    arena = [AR[:, i * D:(i + 1) * D] for i in range(3)]
    COLS = p.sb('COLSA', [128, 32], F32)
    XA = p.sb('XA', [128, 2048], F32)
    XB = p.sb('XB', [128, 2048], F32)
    PACC = p.sb('PACC', [128, 2048], F32)
    XAH = p.sb('XAH', [128, D], BF16)
    XAHF = p.sb('XAHF', [128, NCH * T], BF16)
    QI8 = p.sb('QI8', [128, D], FP8)
    QI8B = p.sb('QI8B', [128, D], BF16)
    QI8F = p.sb('QI8F', [128, NCH * T], BF16)
    Q1F = p.sb('Q1F', [128, 4 * T], BF16)
    K1F = p.sb('K1F', [128, 4 * T], BF16)
    K1TM = p.sb('K1TM', [128, 2 * 512], BF16)
    VT1 = p.sb('VT1', [128, 2 * 512], BF16)
    ATM1 = p.sb('ATM1', [128, NH1 * 128], BF16)
    OCB = p.sb('OCB', [128, 512], BF16)
    OF = p.sb('OF', [128, 4 * T], BF16)
    p.taps = dict(XA=XA, XB=XB, PACC=PACC, Q1F=Q1F, K1F=K1F, K1TM=K1TM,
                  VT1=VT1, OF=OF, ATM1=ATM1, XAHF=XAHF, QI8F=QI8F, COLSA=COLS)
    x1_d, xb_d, pacc_d = io['x1'], io['xb'], io['pacc']

    p.wait('sync', 'sV', p.val('sV'))

    def ld0(y):
        h, _ = p.inc('sX', 16)
        y.dma_start(XT[:, 0:2048].rearrange("q (w f) -> q w f", w=2),
                    x1_d[0:T, :].rearrange("(w q) f -> q w f", q=128)).then_inc(h, 16)
    p.Sy(ld0)
    xload_base = p.val('sX') - 16
    sS_base = p.val('sS')
    st_cnt = {}

    for t in range(NT_DEV):
        buf = t % 2
        X1t = XT[:, buf * 2048:(buf + 1) * 2048]
        if t + 1 < NT_DEV:
            if t - 1 >= 0:
                p.wait('sync', 'sV', st_cnt[t - 1])
            def ldn(y, t=t):
                h, _ = p.inc('sX', 16)
                y.dma_start(XT[:, ((t + 1) % 2) * 2048:((t + 1) % 2 + 1) * 2048]
                            .rearrange("q (w f) -> q w f", w=2),
                            x1_d[(t + 1) * T:(t + 2) * T, :]
                            .rearrange("(w q) f -> q w f", q=128)).then_inc(h, 16)
            p.Sy(ldn)

        for e in ('vector', 'scalar', 'tensor'):
            p.wait(e, 'sX', xload_base + 16 * (t + 1))
        if t > 0:
            for e in ('vector', 'scalar', 'tensor'):
                p.wait(e, 'sS', sS_base + 32 * t)

        for w in range(2):
            cc = 2 * t + w
            x1w = X1t[:, w * 1024:(w + 1) * 1024]
            pX, pXn = (A0, 'A') if w == 0 else (B0, 'B')
            pY, pYn = (B0, 'B') if w == 0 else (A0, 'A')
            pXb = A0b if w == 0 else B0b

            st = ln_stats_cols(p, x1w, COLS, 8 * w, arena[0], c['EPSC'])
            p.wait('vector', 'sA')

            def mkxn(v, x1w=x1w, st=st):
                h, _ = p.inc('sV')
                v.tensor_scalar(arena[1], x1w, st['m'], st['r'], ALU.subtract,
                                ALU.mult).then_inc(h, 1)
            p.Ve(mkxn)
            p.wait('vector', 'sV')

            def amx(v):
                h, _ = p.inc('sV')
                v.reduce_max(COLS[:, 16:17], arena[1], AX.X,
                             apply_absolute_value=True).then_inc(h, 1)
            p.Ve(amx)
            p.wait('vector', 'sV')

            def amx2(v):
                h, _ = p.inc('sV')
                v.tensor_scalar(COLS[:, 16:17], COLS[:, 16:17], 1e-12, None,
                                ALU.max).then_inc(h, 1)
            p.Ve(amx2)
            p.wait('vector', 'sV')

            def amx3(v):
                h, _ = p.inc('sV')
                v.reciprocal(COLS[:, 17:18], COLS[:, 16:17]).then_inc(h, 1)
            p.Ve(amx3)
            p.wait('vector', 'sV')

            def amx4(v):
                v.tensor_scalar(COLS[:, 18:19], COLS[:, 16:17], s['WSC'][:, 0:1],
                                1.0 / 224.0, ALU.mult, ALU.mult)
                h, _ = p.inc('sV')
                v.tensor_scalar(COLS[:, 17:18], COLS[:, 17:18], 224.0, None,
                                ALU.mult).then_inc(h, 1)
            p.Ve(amx4)
            p.wait('scalar', 'sV')

            def q8a(a):
                a.activation(QI8, arena[1], AF.Identity, scale=COLS[:, 17:18])
                h, _ = p.inc('sA')
                a.activation(QI8B, QI8, AF.Identity).then_inc(h, 1)
            p.Ae(q8a)
            p.wait('tensor', 'sA')

            p.guard(pXn)

            def bkmm(t_, cc=cc, pX=pX):
                ins = None
                for g in range(2):
                    ins = t_.matmul(pX[:, 1024 + g * 512:1024 + (g + 1) * 512],
                                    s['BKL'][0:3, cc * 128:(cc + 1) * 128],
                                    s['WBK3'][:, g * 512:(g + 1) * 512],
                                    start=True, stop=True)
                h, _ = p.inc('sP')
                ins.then_inc(h, 1)
            p.Te(bkmm)
            p.wait('vector', 'sP')

            def xav(v, w=w, pX=pX):
                h, _ = p.inc('sV')
                v.tensor_tensor(XA[:, w * 1024:(w + 1) * 1024], arena[1],
                                pX[:, 1024:2048], ALU.add).then_inc(h, 1)
            p.Ve(xav)
            p.mark(pXn, 'sV')

            st2 = ln_stats_cols(p, XA[:, w * 1024:(w + 1) * 1024], COLS,
                                8 * w + 4 if False else 20 + 4 * w, arena[0],
                                c['EPSC'])
            p.wait('scalar', 'sV')

            def xah(a, w=w, st2=st2):
                h, _ = p.inc('sA')
                a.activation(XAH, XA[:, w * 1024:(w + 1) * 1024], AF.Identity,
                             scale=st2['r'], bias=st2['negmr']).then_inc(h, 1)
            p.Ae(xah)
            p.wait('tensor', 'sA')
            tm_to_fm(p, c, XAH, XAHF, w, pX, pXb[:, 0:1024])
            p.wait('tensor', 'sA')
            tm_to_fm(p, c, QI8B, QI8F, w, pX, pXb[:, 0:1024])
            p.wait('tensor', 'sA')

            p.guard(pYn)

            def y8mm(t_, w=w, pY=pY):
                ins = None
                for g in range(2):
                    for k in range(NCH):
                        ins = t_.matmul(pY[:, g * 512:(g + 1) * 512],
                                        QI8F[:, k * T + w * 128:k * T + w * 128 + 128],
                                        s['W8D'][:, k * D + g * 512:k * D + (g + 1) * 512],
                                        start=(k == 0), stop=(k == NCH - 1))
                h, _ = p.inc('sP')
                ins.then_inc(h, 1)
            p.Te(y8mm)
            p.wait('scalar', 'sP')

            def y8ev(a, pY=pY):
                h, _ = p.inc('sA')
                a.activation(arena[2], pY[:, 0:1024], AF.Identity,
                             scale=COLS[:, 18:19]).then_inc(h, 1)
            p.Ae(y8ev)
            p.mark(pYn, 'sA')
            p.wait('vector', 'sA')

            def pacv(v, w=w, x1w=x1w):
                v.tensor_tensor(PACC[:, w * 1024:(w + 1) * 1024], x1w,
                                s['B8BC'], ALU.add)
                h, _ = p.inc('sV')
                v.tensor_tensor(PACC[:, w * 1024:(w + 1) * 1024],
                                PACC[:, w * 1024:(w + 1) * 1024], arena[2],
                                ALU.add).then_inc(h, 1)
            p.Ve(pacv)
            p.wait('tensor', 'sV')

        p.wait('tensor', 'sA')
        p.guard('A')

        def qmm(t_):
            ins = None
            for m in range(4):
                for k in range(NCH):
                    ins = t_.matmul(A0[:, m * T:(m + 1) * T],
                                    s['WQKV'][:, k * 1536 + m * 128:k * 1536 + m * 128 + 128],
                                    XAHF[:, k * T:(k + 1) * T],
                                    start=(k == 0), stop=(k == NCH - 1))
            h, _ = p.inc('sP')
            ins.then_inc(h, 1)
        p.Te(qmm)
        p.wait('scalar', 'sP')

        def qev(a):
            ins = None
            for m in range(4):
                ins = a.activation(Q1F[:, m * T:(m + 1) * T], A0[:, m * T:(m + 1) * T],
                                   AF.Identity, bias=s['BQC'][:, m:m + 1])
            h, _ = p.inc('sA')
            ins.then_inc(h, 1)
        p.Ae(qev)
        p.mark('A', 'sA')
        p.guard('B')

        def kmm(t_):
            ins = None
            for m in range(4):
                for k in range(NCH):
                    ins = t_.matmul(B0[:, m * T:(m + 1) * T],
                                    s['WQKV'][:, k * 1536 + 512 + m * 128:k * 1536 + 512 + m * 128 + 128],
                                    XAHF[:, k * T:(k + 1) * T],
                                    start=(k == 0), stop=(k == NCH - 1))
            h, _ = p.inc('sP')
            ins.then_inc(h, 1)
        p.Te(kmm)
        p.wait('scalar', 'sP')

        def kev(a):
            ins = None
            for m in range(4):
                ins = a.activation(K1F[:, m * T:(m + 1) * T], B0[:, m * T:(m + 1) * T],
                                   AF.Identity, bias=s['BQC'][:, 4 + m:5 + m])
            h, _ = p.inc('sA')
            ins.then_inc(h, 1)
        p.Ae(kev)
        p.mark('B', 'sA')
        p.guard('A')

        def ktmm(t_):
            ins = None
            for w in range(2):
                for k in range(NCH):
                    ins = t_.matmul(A0[:, w * 512:(w + 1) * 512],
                                    XAHF[:, k * T + w * 128:k * T + w * 128 + 128],
                                    s['WQKV'][:, k * 1536 + 512:k * 1536 + 1024],
                                    start=(k == 0), stop=(k == NCH - 1))
            h, _ = p.inc('sP')
            ins.then_inc(h, 1)
        p.Te(ktmm)
        p.wait('vector', 'sP')

        def ktev(v):
            h, _ = p.inc('sV')
            v.tensor_tensor(K1TM.rearrange("q (a b) -> q a b", a=2),
                            A0[:, 0:1024].rearrange("q (a b) -> q a b", a=2),
                            s['BQVBC'][:, 0:512].unsqueeze(1)
                            .broadcast_to([128, 2, 512]), ALU.add).then_inc(h, 1)
        p.Ve(ktev)
        p.mark('A', 'sV')
        p.guard('B')

        def vtmm(t_):
            ins = None
            for w in range(2):
                for k in range(NCH):
                    ins = t_.matmul(B0[:, w * 512:(w + 1) * 512],
                                    XAHF[:, k * T + w * 128:k * T + w * 128 + 128],
                                    s['WQKV'][:, k * 1536 + 1024:k * 1536 + 1536],
                                    start=(k == 0), stop=(k == NCH - 1))
            h, _ = p.inc('sP')
            ins.then_inc(h, 1)
        p.Te(vtmm)
        p.wait('vector', 'sP')

        def vtev(v):
            h, _ = p.inc('sV')
            v.tensor_tensor(AR[:, 2 * D:3 * D].rearrange("q (a b) -> q a b", a=2),
                            B0[:, 0:1024].rearrange("q (a b) -> q a b", a=2),
                            s['BQVBC'][:, 512:1024].unsqueeze(1)
                            .broadcast_to([128, 2, 512]), ALU.add).then_inc(h, 1)
        p.Ve(vtev)
        p.mark('B', 'sV')
        p.wait('scalar', 'sV')

        def vsc(a, t=t):
            a.activation(VT1[:, 0:512], AR[:, 2 * D:3 * D][:, 0:512], AF.Identity,
                         scale=s['EHNE2'][:, 2 * t:2 * t + 1])
            h, _ = p.inc('sA')
            a.activation(VT1[:, 512:1024], AR[:, 2 * D:3 * D][:, 512:1024],
                         AF.Identity,
                         scale=s['EHNE2'][:, 2 * t + 1:2 * t + 2]).then_inc(h, 1)
        p.Ae(vsc)
        p.wait('vector', 'sA')

        for w in range(2):
            cc = 2 * t + w
            pX, pXn = (A0, 'A') if w == 0 else (B0, 'B')
            pY, pYn = (B0, 'B') if w == 0 else (A0, 'A')
            pYb = B0b if w == 0 else A0b

            def hpre(v, cc=cc):
                h, _ = p.inc('sV')
                v.tensor_scalar(s['SPRE1'], s['SST1'], s['DCC2'][:, cc:cc + 1],
                                None, ALU.mult).then_inc(h, 1)
            p.Ve(hpre)
            p.wait('scalar', 'sV')

            def hcast(a):
                sp3 = s['SPRE1'].rearrange("p (h x) -> p h x", x=64)
                a.activation(s['SPB'][0:64, :].rearrange("p (h x) -> p h x", x=64),
                             sp3[:, 0:NH1:2, :], AF.Identity)
                h, _ = p.inc('sA')
                a.activation(s['SPB'][64:128, :].rearrange("p (h x) -> p h x", x=64),
                             sp3[:, 1:NH1:2, :], AF.Identity).then_inc(h, 1)
            p.Ae(hcast)
            p.wait('tensor', 'sA')
            p.guard(pYn)

            def hatme(t_, pY=pY, w=w):
                ins = None
                for h_ in range(0, NH1, 2):
                    col = (h_ // 2) * T + w * 128
                    ins = t_.matmul(pY[:, h_ * 128:(h_ + 1) * 128],
                                    K1F[0:64, col:col + 128],
                                    Q1F[0:64, col:col + 128],
                                    start=True, stop=True)
                hh, _ = p.inc('sP')
                ins.then_inc(hh, 1)
            p.Te(hatme)
            p.guard(pXn, 'tensor')

            def hkv(t_, pX=pX, w=w):
                ins = None
                for h_ in range(NH1):
                    ins = t_.matmul(pX[0:64, 1024 + h_ * 64:1024 + (h_ + 1) * 64],
                                    K1TM[:, w * 512 + h_ * 64:w * 512 + (h_ + 1) * 64],
                                    VT1[:, w * 512 + h_ * 64:w * 512 + (h_ + 1) * 64],
                                    start=True, stop=True)
                hh, _ = p.inc('sP')
                ins.then_inc(hh, 1)
            p.Te(hkv)

            def hatmo(t_, pY=pY, w=w):
                ins = None
                for h_ in range(1, NH1, 2):
                    col = (h_ // 2) * T + w * 128
                    ins = t_.matmul(pY[:, h_ * 128:(h_ + 1) * 128],
                                    K1F[64:128, col:col + 128],
                                    Q1F[64:128, col:col + 128],
                                    start=True, stop=True)
                hh, _ = p.inc('sP')
                ins.then_inc(hh, 1)
            p.Te(hatmo)
            p.wait('vector', 'sP')

            def hmask(v, pY=pY):
                src = pY[:, 0:1024].rearrange("q (h t) -> q h t", t=128)
                msk = c['CAUS'].unsqueeze(1).broadcast_to([128, NH1, 128])
                dst = ATM1.rearrange("q (h t) -> q h t", t=128)
                h, _ = p.inc('sV')
                v.tensor_tensor(dst, src, msk, ALU.mult).then_inc(h, 1)
            p.Ve(hmask)
            p.mark(pYn, 'sV')
            p.wait('tensor', 'sV')

            def homm(t_, pX=pX, w=w):
                ins = None
                for h_ in range(NH1):
                    base = 64 * (h_ % 2)
                    col = (h_ // 2) * T + w * 128
                    t_.matmul(pX[:, h_ * 64:(h_ + 1) * 64],
                              ATM1[:, h_ * 128:(h_ + 1) * 128],
                              VT1[:, w * 512 + h_ * 64:w * 512 + (h_ + 1) * 64],
                              start=True, stop=False)
                    ins = t_.matmul(pX[:, h_ * 64:(h_ + 1) * 64],
                                    Q1F[base:base + 64, col:col + 128],
                                    s['SPB'][base:base + 64,
                                             (h_ // 2) * 64:(h_ // 2) * 64 + 64],
                                    start=False, stop=True)
                hh, _ = p.inc('sP')
                ins.then_inc(hh, 1)
            p.Te(homm)
            p.wait('scalar', 'sP')

            def hoev(a, pX=pX, cc=cc):
                h, _ = p.inc('sA')
                a.activation(OCB, pX[:, 0:512], AF.Identity,
                             scale=s['EHP2'][:, cc:cc + 1]).then_inc(h, 1)
            p.Ae(hoev)
            p.wait('vector', 'sA')
            p.wait('vector', 'sP')

            def hsst(v, pX=pX):
                h, _ = p.inc('sV')
                v.tensor_tensor(s['SST1'], s['SPRE1'], pX[0:64, 1024:1536],
                                ALU.add).then_inc(h, 1)
            p.Ve(hsst)
            p.mark(pXn, 'sV')
            p.wait('tensor', 'sV')
            p.guard(pYn)

            def htr(t_, pYb=pYb):
                ins = None
                for f in range(4):
                    ins = t_.transpose(pYb[:, f * 128:(f + 1) * 128],
                                       OCB[:, f * 128:(f + 1) * 128], c['IDBF'])
                h, _ = p.inc('sP')
                ins.then_inc(h, 1)
            p.Te(htr)
            p.wait('scalar', 'sP')

            def hev(a, pYb=pYb, w=w):
                d3 = OF.rearrange("q (f x) -> q f x", x=T)[
                    :, :, w * 128:w * 128 + 128]
                s3 = pYb[:, 0:512].rearrange("q (f x) -> q f x", x=128)
                h, _ = p.inc('sA')
                a.activation(d3, s3, AF.Identity).then_inc(h, 1)
            p.Ae(hev)
            p.mark(pYn, 'sA')

        for w in range(2):
            pX, pXn = (A0, 'A') if w == 0 else (B0, 'B')
            p.guard(pXn)
            p.wait('tensor', 'sA')

            def xbmm(t_, pX=pX, w=w):
                ins = None
                for g in range(2):
                    sl = pX[:, g * 512:(g + 1) * 512]
                    for e in range(4):
                        t_.matmul(sl, OF[:, e * T + w * 128:e * T + w * 128 + 128],
                                  s['WOUT1'][:, e * D + g * 512:e * D + (g + 1) * 512],
                                  start=(e == 0), stop=False)
                    ins = t_.matmul(sl, c['ONESROWB'],
                                    s['BOUTR'][0:1, g * 512:(g + 1) * 512],
                                    start=False, stop=True)
                h, _ = p.inc('sP')
                ins.then_inc(h, 1)
            p.Te(xbmm)
            p.wait('vector', 'sP')

            def xbv(v, w=w, pX=pX):
                h, _ = p.inc('sV')
                v.tensor_tensor(XB[:, w * 1024:(w + 1) * 1024],
                                XA[:, w * 1024:(w + 1) * 1024],
                                pX[:, 0:1024], ALU.add).then_inc(h, 1)
            p.Ve(xbv)
            p.mark(pXn, 'sV')

        st_cnt[t] = p.val('sV')
        p.wait('sync', 'sV', st_cnt[t])

        def tst(y, t=t):
            h, _ = p.inc('sS', 32)
            y.dma_start(xb_d[t * T:(t + 1) * T, :].rearrange("(w q) f -> q w f", q=128),
                        XB.rearrange("q (w f) -> q w f", w=2)).then_inc(h, 16)
            y.dma_start(pacc_d[t * T:(t + 1) * T, :].rearrange("(w q) f -> q w f", q=128),
                        PACC.rearrange("q (w f) -> q w f", w=2)).then_inc(h, 16)
        p.Sy(tst)

    p.wait('sync', 'sS', p.val('sS'))


def host_layout_A(inputs):
    I = inputs
    m = {}
    m['wqkv'] = I['L1_attn_w_qkv']
    m['wout1'] = I['L1_attn_w_out']
    w8 = I['L1_ffn_weight']
    if w8.dtype != np.uint8:
        w8 = w8.view(np.uint8)
    m['w8t'] = np.ascontiguousarray(w8.T)
    m['wscalec'] = np.broadcast_to(
        np.asarray(I['L1_ffn_weight_scale'], np.float32).reshape(1, 1),
        (128, 1)).copy()
    m['bias8r'] = I['L1_ffn_bias'].reshape(1, D)
    m['g2c'] = np.ascontiguousarray(I['L1_attn_ln2_g'].reshape(NCH, 128).T)
    m['b2c'] = np.ascontiguousarray(I['L1_attn_ln2_b'].reshape(NCH, 128).T)
    m['bqkvr'] = I['L1_attn_b_qkv'].reshape(1, 1536)
    m['boutr'] = I['L1_attn_b_out'].reshape(1, D)
    m['wbk3'] = np.ascontiguousarray(np.concatenate(
        [I['L1_attn_b_bk'].reshape(1, D), I['L1_attn_w_bk']], axis=0))
    return m


def build_pA_standalone():
    p = P()
    io = {}
    io['x1'] = p.nc.dram_tensor('x1', [S, D], F32, kind="ExternalInput").ap()
    io['gre'] = p.nc.dram_tensor('gre', [64, 128], F32, kind="ExternalInput").ap()
    io['gim'] = p.nc.dram_tensor('gim', [64, 128], F32, kind="ExternalInput").ap()
    io['ehd'] = p.nc.dram_tensor('ehd', [128, 256], F32, kind="ExternalInput").ap()
    for nm, shp, dt in [('wqkv', [D, 1536], F32), ('wout1', [512, D], F32),
                        ('w8t', [D, D], U8), ('wscalec', [128, 1], F32),
                        ('bias8r', [1, D], F32), ('g2c', [128, NCH], F32),
                        ('b2c', [128, NCH], F32), ('bqkvr', [1, 1536], F32),
                        ('boutr', [1, D], F32), ('wbk3', [3, D], F32)]:
        io[nm] = p.nc.dram_tensor(nm, shp, dt, kind="ExternalInput").ap()
    io['xb'] = p.nc.dram_tensor('xb', [S, D], F32, kind="ExternalOutput").ap()
    io['pacc'] = p.nc.dram_tensor('pacc', [S, D], F32, kind="ExternalOutput").ap()
    with p.ctx:
        p.psA = p.ps('psA', [128, 2048], F32)
        p.psB = p.ps('psB', [128, 2048], F32)
        c = alloc_consts(p)
        p.open_block()
        emit_consts(p, c)
        for e in ('tensor', 'vector', 'scalar', 'sync'):
            p.wait(e, 'sG')
        s = emit_phaseA_setup(p, c, io)
        emit_phaseA(p, c, io, s)
        if os.environ.get('K_TAP', '0') == '1':
            taps = [('tXA', 'XA', F32), ('tQ1F', 'Q1F', BF16),
                    ('tK1F', 'K1F', BF16), ('tK1TM', 'K1TM', BF16),
                    ('tVT1', 'VT1', BF16), ('tOF', 'OF', BF16),
                    ('tATM1', 'ATM1', BF16), ('tXAHF', 'XAHF', BF16)]
            tio = {}
            for nm, key, dt in taps:
                shp = list(p.taps[key].shape)
                tio[nm] = p.nc.dram_tensor(nm, shp, dt, kind="ExternalOutput").ap()

            def tpst(y):
                h, _ = p.inc('sS', 16 * len(taps))
                for nm, key, dt in taps:
                    y.dma_start(tio[nm][:], p.taps[key]).then_inc(h, 16)
            p.Sy(tpst)
            p.wait('sync', 'sS', p.val('sS'))
        p.close_block()
    return p.nc




FF = 4096


def emit_phaseB(p, c, io):
    A0, B0 = p.psA, p.psB
    A0b = A0.bitcast(BF16)
    W1 = p.sb('W1', [128, NCH * FF], BF16)
    W2 = p.sb('W2', [128, 32 * D], BF16)
    B3CB = p.sb('B3CB', [128, NCH], BF16)
    BF1R = p.sb('BF1R', [1, FF], BF16)
    BF2R = p.sb('BF2R', [1, D], BF16)
    G3C = p.sb('G3C', [128, NCH], F32)
    B1C = p.sb('B1C', [128, 32], F32)
    XBT = p.sb('XBT', [128, 2 * 2048], F32)
    PCT = p.sb('PCT', [128, 2 * 2048], F32)
    XBH = p.sb('XBH', [128, 4 * D], BF16)
    XBHF = p.sb('XBHF', [128, NCH * T], BF16)
    G1F = p.sb('G1F', [128, 8 * T], BF16)
    ARB = p.sb('ARB', [128, D], F32)
    COLS = p.sb('COLSB', [128, 16], F32)

    p.sem('sWD'); p.sem('sX'); p.sem('sS')

    def wload(g):
        h = p.sem('sWD')
        g.dma_start(W1.rearrange("q (c n) -> q c n", c=NCH),
                    io['wff1'].rearrange("(c q) n -> q c n", q=128)).then_inc(h, 16)
        g.dma_start(W2.rearrange("q (c n) -> q c n", c=32),
                    io['wff2'].rearrange("(c q) n -> q c n", q=128)).then_inc(h, 16)
        g.dma_start(B3CB, io['b3c'][:]).then_inc(h, 16)
        g.dma_start(BF1R, io['bff1r'][:]).then_inc(h, 16)
        g.dma_start(BF2R, io['bff2r'][:]).then_inc(h, 16)
    p.cnt['sWD'] += 16 * 5
    p.Ge(wload)

    def wload2(y):
        h, _ = p.inc('sX', 16)
        y.dma_start(G3C, io['g3c'][:]).then_inc(h, 16)
    p.Sy(wload2)

    for e in ('tensor', 'vector', 'scalar'):
        p.wait(e, 'sG')
        p.wait(e, 'sWD')
        p.wait(e, 'sX')

    for half in range(2):
        p.push_scope()
        B1R = p.sb(f'B1R{half}', [1, FF // 2], F32)
        p.guard('A')

        def b1mm(t_, half=half):
            ins = None
            for g in range(4):
                seg = half * 2048 + g * 512
                for k in range(NCH):
                    t_.matmul(A0[0:1, g * 512:(g + 1) * 512], B3CB[:, k:k + 1],
                              W1[:, k * FF + seg:k * FF + seg + 512],
                              start=(k == 0), stop=False)
                ins = t_.matmul(A0[0:1, g * 512:(g + 1) * 512],
                                c['ONESROWB'][0:1, 0:1],
                                BF1R[0:1, seg:seg + 512], start=False, stop=True)
            h, _ = p.inc('sP')
            ins.then_inc(h, 1)
        p.Te(b1mm)
        p.wait('scalar', 'sP')

        def b1ev(a, B1R=B1R):
            h, _ = p.inc('sA')
            a.activation(B1R, A0[0:1, 0:2048], AF.Identity).then_inc(h, 1)
        p.Ae(b1ev)
        p.mark('A', 'sA')
        p.wait('tensor', 'sA')
        p.guard('B')

        def b1t(t_, B1R=B1R):
            ins = None
            for j in range(16):
                ins = t_.transpose(B0[:, j:j + 1], B1R[0:1, j * 128:(j + 1) * 128],
                                   c['IDF32'][0:1, 0:1])
            h, _ = p.inc('sP')
            ins.then_inc(h, 1)
        p.Te(b1t)
        p.wait('scalar', 'sP')

        def b1ce(a, half=half):
            h, _ = p.inc('sA')
            a.activation(B1C[:, half * 16:(half + 1) * 16], B0[:, 0:16],
                         AF.Identity).then_inc(h, 1)
        p.Ae(b1ce)
        p.mark('B', 'sA')
        p.wait('tensor', 'sA')
        p.pop_scope()

    def g3f(a):
        ins = None
        for k in range(NCH):
            ins = a.activation(W1[:, k * FF:(k + 1) * FF],
                               W1[:, k * FF:(k + 1) * FF],
                               AF.Identity, scale=G3C[:, k:k + 1])
        h, _ = p.inc('sA')
        ins.then_inc(h, 1)
    p.Ae(g3f)
    p.wait('tensor', 'sA')
    p.wait('sync', 'sA')

    xb_d, pacc_d, out_d = io['xb'], io['pacc'], io['out']
    XBHF2 = p.sb('XBHF2', [128, NCH * T], BF16)
    XBHFs = [XBHF, XBHF2]
    p.wait('tensor', 'sA')

    def ld0(y):
        h, _ = p.inc('sX', 64)
        for tt in range(2):
            if tt >= NT_DEV:
                p.cnt['sX'] -= 32
                break
            y.dma_start(XBT[:, tt * 2048:(tt + 1) * 2048].rearrange("q (w f) -> q w f", w=2),
                        xb_d[tt * T:(tt + 1) * T, :].rearrange("(w q) f -> q w f", q=128)).then_inc(h, 16)
            y.dma_start(PCT[:, tt * 2048:(tt + 1) * 2048].rearrange("q (w f) -> q w f", w=2),
                        pacc_d[tt * T:(tt + 1) * T, :].rearrange("(w q) f -> q w f", q=128)).then_inc(h, 16)
    p.Sy(ld0)
    xload_base = p.val('sX') - 64
    sS_base = p.val('sS')
    st_cnt = {}

    def xneed(t):
        return xload_base + (32 if t == 0 else 64 + 32 * (t - 1))

    def prep(t):
        XBt_ = XBT[:, (t % 2) * 2048:(t % 2 + 1) * 2048]
        for e in ('vector', 'scalar'):
            p.wait(e, 'sX', xneed(t))
        for w in range(2):
            xw = XBt_[:, w * 1024:(w + 1) * 1024]
            st = ln_stats_cols(p, xw, COLS, 8 * w, ARB, c['EPSC'])
            p.wait('scalar', 'sV')

            def mkxn(a, xw=xw, st=st, w=w, t=t):
                h, _ = p.inc('sA')
                a.activation(XBH[:, (t % 2) * 2048 + w * 1024:
                                 (t % 2) * 2048 + (w + 1) * 1024],
                             xw, AF.Identity,
                             scale=st['r'], bias=st['negmr']).then_inc(h, 1)
            p.Ae(mkxn)

    def ptrans(t):
        p.wait('tensor', 'sA')
        p.guard('C', 'tensor')
        Ab = A0.bitcast(BF16)
        dst = XBHFs[t % 2]

        def ft(t_, t=t, Ab=Ab):
            ins = None
            for i in range(16):
                ins = t_.transpose(Ab[:, i * 128:(i + 1) * 128],
                                   XBH[:, (t % 2) * 2048 + i * 128:
                                       (t % 2) * 2048 + (i + 1) * 128], c['IDBF'])
            h, _ = p.inc('sP')
            ins.then_inc(h, 1)
        p.Te(ft)
        p.wait('vector', 'sP')

        def fv(v, dst=dst, Ab=Ab):
            ins = None
            for w in range(2):
                d3 = dst.rearrange("q (k x) -> q k x", x=T)[:, :, w * 128:(w + 1) * 128]
                s3 = Ab[:, w * 1024:(w + 1) * 1024].rearrange(
                    "q (k x) -> q k x", x=128)
                ins = v.tensor_copy(d3, s3)
            h, _ = p.inc('sV')
            ins.then_inc(h, 1)
        p.Ve(fv)
        p.mark('C', 'sV')

    prep(0)
    ptrans(0)
    for t in range(NT_DEV):
        buf = t % 2
        XBt = XBT[:, buf * 2048:(buf + 1) * 2048]
        PCt = PCT[:, buf * 2048:(buf + 1) * 2048]
        XF = XBHFs[buf]
        if t > 0:
            p.wait('tensor', 'sS', sS_base + 16 * t)

        p.guard('B')

        def mk_f2mm(grp):
            Gh = G1F[:, (grp % 2) * 1024:(grp % 2 + 1) * 1024]

            def f2mm(t_, grp=grp, Gh=Gh):
                ins = None
                for w in range(2):
                    for g in range(2):
                        sl = B0[:, w * 1024 + g * 512:w * 1024 + (g + 1) * 512]
                        for j in range(4):
                            fc = grp * 4 + j
                            ins = t_.matmul(
                                sl, Gh[:, j * T + w * 128:j * T + w * 128 + 128],
                                W2[:, fc * D + g * 512:fc * D + (g + 1) * 512],
                                start=(grp == 0 and j == 0), stop=False)
                h, _ = p.inc('sP')
                ins.then_inc(h, 1)
            p.Te(f2mm)

        for grp in range(8):
            half = grp % 2
            pn = 'C' if half == 0 else 'D'
            Ah = A0[:, half * 1024:(half + 1) * 1024]
            Gh = G1F[:, half * 1024:(half + 1) * 1024]
            p.guard(pn)

            def f1mm(t_, grp=grp, Ah=Ah, XF=XF):
                ins = None
                for j in range(4):
                    fc = grp * 4 + j
                    for k in range(NCH):
                        ins = t_.matmul(Ah[:, j * T:(j + 1) * T],
                                        W1[:, k * FF + fc * 128:k * FF + fc * 128 + 128],
                                        XF[:, k * T:(k + 1) * T],
                                        start=(k == 0), stop=(k == NCH - 1))
                h, _ = p.inc('sP')
                ins.then_inc(h, 1)
            p.Te(f1mm)
            p.wait('scalar', 'sP')

            def f1ev(a, grp=grp, Ah=Ah, Gh=Gh):
                ins = None
                for j in range(4):
                    fc = grp * 4 + j
                    ins = a.activation(Gh[:, j * T:(j + 1) * T],
                                       Ah[:, j * T:(j + 1) * T],
                                       AF.Gelu_apprx_tanh,
                                       bias=B1C[:, fc:fc + 1])
                h, _ = p.inc('sA')
                ins.then_inc(h, 1)
            p.Ae(f1ev)
            p.mark(pn, 'sA')
            if grp > 0:
                p.wait('tensor', 'sA', p.val('sA') - 1)
                mk_f2mm(grp - 1)
            if grp == 0 and t + 1 < NT_DEV:
                prep(t + 1)
            if grp == 4 and t + 1 < NT_DEV:
                ptrans(t + 1)
        p.wait('tensor', 'sA')
        mk_f2mm(7)

        def f2b(t_):
            ins = None
            for w in range(2):
                for g in range(2):
                    ins = t_.matmul(B0[:, w * 1024 + g * 512:w * 1024 + (g + 1) * 512],
                                    c['ONESROWB'], BF2R[0:1, g * 512:(g + 1) * 512],
                                    start=False, stop=True)
            h, _ = p.inc('sP')
            ins.then_inc(h, 1)
        p.Te(f2b)
        p.wait('vector', 'sP')
        p.wait('vector', 'sX', xneed(t))

        def outv(v, XBt=XBt, PCt=PCt):
            v.tensor_tensor(XBt[:, 0:1024], XBt[:, 0:1024], B0[:, 0:1024], ALU.add)
            v.tensor_tensor(XBt[:, 1024:2048], XBt[:, 1024:2048],
                            B0[:, 1024:2048], ALU.add)
            v.tensor_tensor(XBt[:, 0:1024], XBt[:, 0:1024], PCt[:, 0:1024], ALU.add)
            h, _ = p.inc('sV')
            v.tensor_tensor(XBt[:, 1024:2048], XBt[:, 1024:2048],
                            PCt[:, 1024:2048], ALU.add).then_inc(h, 1)
        p.Ve(outv)
        p.mark('B', 'sV')
        st_cnt[t] = p.val('sV')
        p.wait('sync', 'sV', st_cnt[t])

        def tst(y, t=t, XBt=XBt):
            h, _ = p.inc('sS', 16)
            y.dma_start(out_d[t * T:(t + 1) * T, :].rearrange("(w q) f -> q w f", q=128),
                        XBt.rearrange("q (w f) -> q w f", w=2)).then_inc(h, 16)
        p.Sy(tst)

        if t + 2 < NT_DEV:
            def ldn(y, t=t):
                nb = t % 2
                h, _ = p.inc('sX', 32)
                y.dma_start(XBT[:, nb * 2048:(nb + 1) * 2048]
                            .rearrange("q (w f) -> q w f", w=2),
                            xb_d[(t + 2) * T:(t + 3) * T, :]
                            .rearrange("(w q) f -> q w f", q=128)).then_inc(h, 16)
                y.dma_start(PCT[:, nb * 2048:(nb + 1) * 2048]
                            .rearrange("q (w f) -> q w f", w=2),
                            pacc_d[(t + 2) * T:(t + 3) * T, :]
                            .rearrange("(w q) f -> q w f", q=128)).then_inc(h, 16)
            p.Sy(ldn)

    p.wait('sync', 'sS', p.val('sS'))


def host_layout_B(inputs):
    I = inputs
    m = {}
    m['wff1'] = I['L1_attn_w_ff1']
    m['wff2'] = I['L1_attn_w_ff2']
    m['g3c'] = np.ascontiguousarray(I['L1_attn_ln3_g'].reshape(NCH, 128).T)
    m['b3c'] = np.ascontiguousarray(I['L1_attn_ln3_b'].reshape(NCH, 128).T)
    m['bff1r'] = I['L1_attn_b_ff1'].reshape(1, FF)
    m['bff2r'] = I['L1_attn_b_ff2'].reshape(1, D)
    return m


def build_pB_standalone():
    p = P()
    io = {}
    io['xb'] = p.nc.dram_tensor('xb', [S, D], F32, kind="ExternalInput").ap()
    io['pacc'] = p.nc.dram_tensor('pacc', [S, D], F32, kind="ExternalInput").ap()
    for nm, shp in [('wff1', [D, FF]), ('wff2', [FF, D]), ('g3c', [128, NCH]),
                    ('b3c', [128, NCH]), ('bff1r', [1, FF]), ('bff2r', [1, D])]:
        io[nm] = p.nc.dram_tensor(nm, shp, F32, kind="ExternalInput").ap()
    io['out'] = p.nc.dram_tensor('out', [S, D], F32, kind="ExternalOutput").ap()
    with p.ctx:
        p.psA = p.ps('psA', [128, 2048], F32)
        p.psB = p.ps('psB', [128, 2048], F32)
        c = alloc_consts(p)
        p.open_block()
        emit_consts(p, c)
        for e in ('tensor', 'vector', 'scalar', 'sync'):
            p.wait(e, 'sG')
        emit_phaseB(p, c, io)
        p.close_block()
    return p.nc



def host_layout(inputs, core):
    I = inputs
    m = {}
    m['x'] = np.ascontiguousarray(I['L0_attn_x'][core])
    m['wq'] = I['L0_attn_Wq']
    m['wk'] = I['L0_attn_Wk']
    m['wv'] = I['L0_attn_Wv']
    m['wg'] = I['L0_attn_Wg']
    m['wout'] = I['L0_attn_Wout']
    m['wffT'] = np.ascontiguousarray(I['L0_ffn_weight'].T)
    m['wgk1'] = I['L0_attn_Wgk1']
    m['wgk2'] = I['L0_attn_Wgk2']
    m['bgk2r'] = I['L0_attn_bgk2'].reshape(1, D)
    m['g0r'] = I['L0_attn_gamma'].reshape(1, D)
    m['b0r'] = I['L0_attn_beta'].reshape(1, D)
    m['disha'] = I['L0_ffn_disha']
    m['wp0r'] = np.ascontiguousarray(I['L1_attn_w_pot'][:, 0]).reshape(1, D)
    m['wp1r'] = np.ascontiguousarray(I['L1_attn_w_pot'][:, 1]).reshape(1, D)
    m['bpotr'] = I['L1_attn_b_pot'].reshape(1, 2)
    m['g1r'] = I['L1_attn_ln1_g'].reshape(1, D)
    m['b1r'] = I['L1_attn_ln1_b'].reshape(1, D)
    return m


def build_p0_standalone():
    p = P()
    io = {}
    io['x'] = p.nc.dram_tensor('x', [S, D], F32, kind="ExternalInput").ap()
    for nm, shp in [('wq', [D, D]), ('wk', [D, D]), ('wv', [D, D]), ('wg', [D, D]),
                    ('wout', [D, D]), ('wffT', [D, D]), ('wgk1', [D, 16]),
                    ('wgk2', [16, D]), ('disha', [64, D])]:
        io[nm] = p.nc.dram_tensor(nm, shp, F32, kind="ExternalInput").ap()
    for nm, shp in [('bgk2r', [1, D]), ('g0r', [1, D]), ('b0r', [1, D]),
                    ('wp0r', [1, D]), ('wp1r', [1, D]), ('bpotr', [1, 2]),
                    ('g1r', [1, D]), ('b1r', [1, D])]:
        io[nm] = p.nc.dram_tensor(nm, shp, F32, kind="ExternalInput").ap()
    io['x1'] = p.nc.dram_tensor('x1', [S, D], F32, kind="ExternalOutput").ap()
    vg_d = p.nc.dram_tensor('vg', [128, 128], F32, kind="ExternalOutput").ap()

    with p.ctx:
        p.psA = p.ps('psA', [128, 2048], F32)
        p.psB = p.ps('psB', [128, 2048], F32)
        c = alloc_consts(p)
        VT = p.sb('VT', [128, NCHUNK], F32)
        GT = p.sb('GT', [128, NCHUNK], F32)
        p.open_block()
        def vg0(g):
            h, _ = p.inc('sG')
            g.memset(VT, 0.0)
            g.memset(GT, 0.0).then_inc(h, 1)
        p.Ge(vg0)
        emit_consts(p, c)
        for e in ('tensor', 'vector', 'scalar', 'sync'):
            p.wait(e, 'sG')
        emit_phase0(p, c, io, VT, GT)

        if os.environ.get('K_TAP'):
            tap_d = p.nc.dram_tensor('tap', [128, 24576], F32, kind="ExternalOutput").ap()
            p.join(engines=('sync',))
            def tapst(y):
                h = p.sem('sS')
                col = 0
                for name in ('XNF', 'QFM', 'KFM', 'KTM', 'VTM', 'GATE', 'GKT',
                             'ATM', 'RFM', 'X1', 'AR', 'COLS', 'PL1'):
                    tsr = p.taps[name]
                    wid_b = tsr.ap[-1][1] * (2 if tsr.dtype == BF16 else 4)
                    wid = wid_b // 4
                    y.dma_start(tap_d[:, col:col + wid],
                                tsr.bitcast(F32) if tsr.dtype == BF16 else tsr
                                ).then_inc(h, 16)
                    col += wid
                for name in ('GWBC2',):
                    tsr = p.taps[name]
                    y.dma_start(tap_d[:, col:col + 1024],
                                tsr.bitcast(F32)).then_inc(h, 16)
                    col += 1024
                for name in ('GBC', 'BBC'):
                    tsr = p.taps[name]
                    y.dma_start(tap_d[:, col:col + 1024], tsr).then_inc(h, 16)
                    col += 1024
                y.dma_start(tap_d[:, col:col + 2], p.taps['BWC']).then_inc(h, 16)
                col += 2
                for name in ('SPRE', 'SST'):
                    tsr = p.taps[name]
                    y.dma_start(tap_d[0:64, col:col + 1024], tsr).then_inc(h, 16)
                    col += 1024
            p.Sy(tapst)
            p.cnt['sS'] += 16 * 19

        p.wait('sync', 'sV')

        def vgst(y):
            h = p.sem('sS')
            y.dma_start(vg_d[:, 0:64], VT).then_inc(h, 16)
            y.dma_start(vg_d[:, 64:128], GT).then_inc(h, 16)
        p.Sy(vgst)
        p.cnt['sS'] += 32
        p.wait('sync', 'sS')
        p.close_block()
    return p.nc


def build_scan_standalone():
    p = P()
    io = {}
    io['vg'] = p.nc.dram_tensor('vg', [128, 128], F32, kind="ExternalInput").ap()
    io['gre'] = p.nc.dram_tensor('gre', [64, 128], F32, kind="ExternalOutput").ap()
    io['gim'] = p.nc.dram_tensor('gim', [64, 128], F32, kind="ExternalOutput").ap()
    eh_d = p.nc.dram_tensor('ehd', [128, 256], F32, kind="ExternalOutput").ap()
    with p.ctx:
        p.psA = p.ps('psA', [128, 2048], F32)
        p.psB = p.ps('psB', [128, 2048], F32)
        c = alloc_consts(p)
        VT = p.sb('VT', [128, NCHUNK], F32)
        GT = p.sb('GT', [128, NCHUNK], F32)
        p.open_block()
        emit_consts(p, c)
        for e in ('tensor', 'vector', 'scalar', 'sync'):
            p.wait(e, 'sG')
        def ld(y):
            h = p.sem('sX')
            y.dma_start(VT, io['vg'][:, 0:64]).then_inc(h, 16)
            y.dma_start(GT, io['vg'][:, 64:128]).then_inc(h, 16)
        p.sem('sX')
        p.Sy(ld)
        p.cnt['sX'] += 32
        for e in ('tensor', 'vector', 'scalar'):
            p.wait(e, 'sX')
        keep = {}
        emit_scan(p, c, io, VT, GT, keep)
        p.wait('sync', 'sA')
        p.wait('sync', 'sV')
        def ehst(y):
            h = p.sem('sDR')
            y.dma_start(eh_d[:, 0:64], keep['EHC']).then_inc(h, 16)
            y.dma_start(eh_d[:, 64:128], keep['EHNE']).then_inc(h, 16)
            y.dma_start(eh_d[:, 128:192], keep['EHP']).then_inc(h, 16)
            y.dma_start(eh_d[0:64, 192:256], keep['DCC']).then_inc(h, 16)
        p.Sy(ehst)
        p.cnt['sDR'] += 64
        if os.environ.get('K_STAP', '0') == '1':
            taps = [('tsre', 'SRE', [128, 260]), ('tsim', 'SIM', [128, 260]),
                    ('ttrr', 'TRR', [1, 1024]), ('tent', 'ENT', [1, 520]),
                    ('tscl', 'SCL', [128, 7]), ('taab', 'AAB', [128, 256]),
                    ('tp0c', 'P0C', [128, 4]), ('tdre', 'DRE', [128, 128]),
                    ('tdim', 'DIM', [128, 128])]
            tio = {nm: p.nc.dram_tensor(nm, shp, F32, kind="ExternalOutput").ap()
                   for nm, _, shp in taps}

            def tst(y):
                h = p.sem('sDR')
                for nm, key, shp in taps:
                    y.dma_start(tio[nm][:], keep[key][0:shp[0], 0:shp[1]]).then_inc(h, 16)
            p.Sy(tst)
            p.cnt['sDR'] += 16 * len(taps)
        p.wait('sync', 'sDR', p.val('sDR'))
        p.close_block()
    return p.nc


def build_fused():
    p = P()
    io = {}
    io['x'] = p.nc.dram_tensor('x', [S, D], F32, kind="ExternalInput").ap()
    for nm, shp in [('wq', [D, D]), ('wk', [D, D]), ('wv', [D, D]), ('wg', [D, D]),
                    ('wout', [D, D]), ('wffT', [D, D]), ('wgk1', [D, 16]),
                    ('wgk2', [16, D]), ('disha', [64, D]),
                    ('bgk2r', [1, D]), ('g0r', [1, D]), ('b0r', [1, D]),
                    ('wp0r', [1, D]), ('wp1r', [1, D]), ('bpotr', [1, 2]),
                    ('g1r', [1, D]), ('b1r', [1, D]),
                    ('wqkv', [D, 1536]), ('wout1', [512, D]),
                    ('wscalec', [128, 1]), ('bias8r', [1, D]),
                    ('g2c', [128, NCH]), ('b2c', [128, NCH]),
                    ('bqkvr', [1, 1536]), ('boutr', [1, D]), ('wbk3', [3, D]),
                    ('wff1', [D, FF]), ('wff2', [FF, D]), ('g3c', [128, NCH]),
                    ('b3c', [128, NCH]), ('bff1r', [1, FF]), ('bff2r', [1, D])]:
        io[nm] = p.nc.dram_tensor(nm, shp, F32, kind="ExternalInput").ap()
    io['w8t'] = p.nc.dram_tensor('w8t', [D, D], U8, kind="ExternalInput").ap()
    for nm, shp in [('x1', [S, D]), ('vg', [128, 128]), ('gre', [64, 128]),
                    ('gim', [64, 128]), ('ehd', [128, 256]),
                    ('xb', [S, D]), ('pacc', [S, D])]:
        io[nm] = p.nc.dram_tensor(nm, shp, F32, kind="Internal").ap()
    io['out'] = p.nc.dram_tensor('out', [S, D], F32, kind="ExternalOutput").ap()
    vg_d = io['vg']
    eh_d = io['ehd']

    with p.ctx:
        p.psA = p.ps('psA', [128, 2048], F32)
        p.psB = p.ps('psB', [128, 2048], F32)
        c = alloc_consts(p)

        p.push_scope()
        VT = p.sb('VT', [128, NCHUNK], F32)
        GT = p.sb('GT', [128, NCHUNK], F32)
        p.open_block()

        def vg0(g):
            h, _ = p.inc('sG')
            g.memset(VT, 0.0)
            g.memset(GT, 0.0).then_inc(h, 1)
        p.Ge(vg0)
        emit_consts(p, c)
        for e in ('tensor', 'vector', 'scalar', 'sync'):
            p.wait(e, 'sG')
        emit_phase0(p, c, io, VT, GT)
        p.wait('sync', 'sV')

        def vgst(y):
            h = p.sem('sS')
            y.dma_start(vg_d[:, 0:64], VT).then_inc(h, 16)
            y.dma_start(vg_d[:, 64:128], GT).then_inc(h, 16)
        p.Sy(vgst)
        p.cnt['sS'] += 32
        p.wait('sync', 'sS')
        p.close_block()
        p.pop_scope()

        p.push_scope()
        VT2 = p.sb('VT2', [128, NCHUNK], F32)
        GT2 = p.sb('GT2', [128, NCHUNK], F32)
        p.open_block()

        def ld(y):
            h = p.sem('sX')
            y.dma_start(VT2, vg_d[:, 0:64]).then_inc(h, 16)
            y.dma_start(GT2, vg_d[:, 64:128]).then_inc(h, 16)
        p.Sy(ld)
        p.cnt['sX'] += 32
        for e in ('tensor', 'vector', 'scalar'):
            p.wait(e, 'sX')
        keep = {}
        emit_scan(p, c, io, VT2, GT2, keep)
        p.wait('sync', 'sA')
        p.wait('sync', 'sV')

        def ehst(y):
            h = p.sem('sDR')
            y.dma_start(eh_d[:, 0:64], keep['EHC']).then_inc(h, 16)
            y.dma_start(eh_d[:, 64:128], keep['EHNE']).then_inc(h, 16)
            y.dma_start(eh_d[:, 128:192], keep['EHP']).then_inc(h, 16)
            y.dma_start(eh_d[0:64, 192:256], keep['DCC']).then_inc(h, 16)
        p.Sy(ehst)
        p.cnt['sDR'] += 64
        p.wait('sync', 'sDR', p.val('sDR'))
        p.close_block()
        p.pop_scope()

        p.push_scope()
        p.open_block()
        s = emit_phaseA_setup(p, c, io)
        emit_phaseA(p, c, io, s)
        p.close_block()
        p.pop_scope()

        p.push_scope()
        p.open_block()
        emit_phaseB(p, c, io)
        p.close_block()
        p.pop_scope()
    return p.nc


def kernel(**inputs):
    cores = list(range(8))
    nc = build_fused()
    mA = host_layout_A(inputs)
    mB = host_layout_B(inputs)
    maps = []
    for core in cores:
        m = host_layout(inputs, core)
        m.update(mA)
        m.update(mB)
        maps.append(m)
    res = run_bass_kernel_spmd(nc, maps, core_ids=cores)
    return np.stack([np.asarray(res.results[i]['out'], np.float32)
                     for i in cores], axis=0)


if __name__ == '__main__':
    nc = build_p0_standalone()
    print("built ok: instructions:",
          sum(1 for _ in nc.all_instructions()))
